# version 11
# baseline (speedup 1.0000x reference)
import numpy as np
import contextlib
import concourse.bass as bass
import concourse.mybir as mybir
from concourse.bass_utils import run_bass_kernel_spmd

F32 = mybir.dt.float32
BF16 = mybir.dt.bfloat16
AF = mybir.ActivationFunctionType
ALU = mybir.AluOpType
AX = mybir.AxisListType

MAGIC = float(np.float32(1.5 * 2 ** 23))
NCORES = 8
S, D, NH, HD, NB, BS = 2048, 1024, 16, 64, 8, 256
NLAT, CH, LH, H1 = 64, 32, 4, 4096

CC_SCL, CC_1_1024, CC_EPS, CC_MAG, CC_NMAG, CC_1_32, CC_ONE = range(7)


class Sem:
    __slots__ = ("h", "n", "name")

    def __init__(self, h, name):
        self.h, self.n, self.name = h, 0, name


class Res:
    __slots__ = ("name", "w", "readers")

    def __init__(self, name):
        self.name, self.w, self.readers = name, None, []


class Prog:
    ENGINES = ("sync", "gpsimd", "vector", "scalar", "tensor")

    def __init__(self, nc, stack):
        self.nc, self.stack = nc, stack
        self.ops = []
        self.esem = {}
        self.res = {}

    def sem(self, name):
        return Sem(self.stack.enter_context(self.nc.semaphore(name)), name)

    def r(self, name):
        if name not in self.res:
            self.res[name] = Res(name)
        return self.res[name]

    ALIASES = {}

    def op(self, eng, fn, reads=(), writes=(), waits=(), sem=None, amt=1):
        writes = list(writes)
        for w in list(writes):
            writes.extend(self.ALIASES.get(w, ()))
        if fn is None:
            assert not reads and not writes
            self.ops.append((eng, None, tuple(waits), None, 0))
            return
        if sem is None:
            sem = self.esem[eng]
        if sem is None:
            sem_ = self.esem[eng]
        else:
            sem_ = sem
        wl = list(waits)
        for rn in reads:
            rr = self.r(rn)
            if rr.w is not None:
                wl.append(rr.w)
        for wn in writes:
            rr = self.r(wn)
            if rr.w is not None and rr.w[0] is not sem_:
                wl.append(rr.w)
            wl.extend(rr.readers)
        sem = sem_
        self.ops.append((eng, fn, tuple(wl), sem, amt))
        sem.n += amt
        ident = (sem, sem.n)
        for rn in reads:
            self.r(rn).readers.append(ident)
        for wn in writes:
            rr = self.r(wn)
            rr.w = ident
            rr.readers = []

    def emit(self, block):
        for eng_name in self.ENGINES:
            ops = [o for o in self.ops if o[0] == eng_name]

            def body(eng, ops=ops):
                waited = {}
                for _, fn, waits, sem, amt in ops:
                    best = {}
                    for s, val in waits:
                        if val > 0 and val > best.get(id(s), (None, -1))[1]:
                            best[id(s)] = (s, val)
                    for s, val in best.values():
                        if waited.get(id(s), -1) < val:
                            eng.wait_ge(s.h, val)
                            waited[id(s)] = val
                    ins = fn(eng) if fn is not None else None
                    if ins is not None and sem is not None:
                        ins.then_inc(sem.h, amt)
            getattr(block, eng_name)(body)


def build_host_consts():
    c = {}
    c["ident"] = np.eye(128, dtype=np.float32)
    i = np.arange(128)
    j2 = np.arange(256)
    m = np.zeros((128, 512), dtype=np.float32)
    m[:, 0:128] = (np.arange(128)[None, :] <= i[:, None])
    m[:, 128:384] = (i[:, None] < j2[None, :]) & (i[:, None] >= j2[None, :] - 128)
    m[:, 384:512] = (np.arange(128, 256)[None, :] >= i[:, None] + 129)
    c["masks"] = m
    m0 = m.copy()
    m0[:, 0:128] = 0.0
    c["masks0"] = m0
    lat = np.zeros((64, NB * 256), dtype=np.float32)
    l = np.arange(NLAT)
    for b in range(NB):
        s_idx = b * BS + np.arange(BS)
        lat[:, b * 256:(b + 1) * 256] = (l[:, None] < (s_idx[None, :] // CH)).astype(np.float32)
    c["latmask"] = lat
    oc = np.zeros((128, 256), dtype=np.float32)
    for h in range(16):
        oc[:, h * 16 + h] = 1.0
    c["onescol"] = oc
    ps = np.zeros((16, 1024), dtype=np.float32)
    for p_ in range(8):
        ps[2 * p_, p_ * 128:p_ * 128 + 64] = 1.0
        ps[2 * p_ + 1, p_ * 128 + 64:p_ * 128 + 128] = 1.0
    c["pairsel"] = ps
    lps = np.zeros((4, 256), dtype=np.float32)
    for p_ in range(2):
        lps[2 * p_, p_ * 128:p_ * 128 + 64] = 1.0
        lps[2 * p_ + 1, p_ * 128 + 64:p_ * 128 + 128] = 1.0
    c["latpairsel"] = lps
    c["onesrow"] = np.ones((1, 128), dtype=np.float32)
    cc = np.zeros((128, 8), dtype=np.float32)
    cc[:, CC_SCL] = 0.125
    cc[:, CC_1_1024] = 1.0 / 1024.0
    cc[:, CC_EPS] = 1e-5
    cc[:, CC_MAG] = MAGIC
    cc[:, CC_NMAG] = -MAGIC
    cc[:, CC_1_32] = 1.0 / 32.0
    cc[:, CC_ONE] = 1.0
    c["cc"] = cc
    return c


def build_weight_inputs(w):
    d = {}
    d["qkv_w"] = np.ascontiguousarray(w["L0_attn_qkv_w"])
    d["qkv_b_pm"] = np.ascontiguousarray(np.asarray(w["L0_attn_qkv_b"]).reshape(24, 128).T)
    d["qkv_b_row"] = np.ascontiguousarray(np.asarray(w["L0_attn_qkv_b"])[None, :])
    d["rq_w"] = np.ascontiguousarray(w["L0_attn_rq_w"])
    d["rq_b_pm"] = np.ascontiguousarray(np.asarray(w["L0_attn_rq_b"]).reshape(2, 128).T)
    d["rkv_w"] = np.ascontiguousarray(w["L0_attn_rkv_w"])
    d["rkv_b_pm"] = np.ascontiguousarray(np.asarray(w["L0_attn_rkv_b"])[0:256].reshape(2, 128).T)
    d["rkv_b_rv_row"] = np.ascontiguousarray(np.asarray(w["L0_attn_rkv_b"])[None, 256:512])
    d["rout_w"] = np.ascontiguousarray(w["L0_attn_rout_w"])
    d["rout_b_pm"] = np.ascontiguousarray(np.asarray(w["L0_attn_rout_b"]).reshape(8, 128).T)
    d["out_w"] = np.ascontiguousarray(w["L0_attn_out_w"])
    d["out_b_row"] = np.ascontiguousarray(np.asarray(w["L0_attn_out_b"])[None, :])
    d["w1t"] = np.ascontiguousarray(np.asarray(w["L0_ffn_w1"]).T)
    d["b1_pm"] = np.ascontiguousarray(np.asarray(w["L0_ffn_b1"]).reshape(32, 128).T)
    d["w2t"] = np.ascontiguousarray(np.asarray(w["L0_ffn_w2"]).T)
    d["b2_row"] = np.ascontiguousarray(np.asarray(w["L0_ffn_b2"])[None, :])
    gl = np.asarray(w["L0_attn_gl_w"])[:, 0]
    gr = np.asarray(w["L0_attn_gr_w"])[:, 0]
    gp = np.zeros((128, 256), dtype=np.float32)
    rp = np.zeros((128, 256), dtype=np.float32)
    for h in range(16):
        par = h % 2
        gp[par * 64:(par + 1) * 64, h * 16 + h] = gl
        rp[par * 64:(par + 1) * 64, h * 16 + h] = gr
    d["glpat"] = gp
    d["grpat"] = rp
    return d


def build_program(nseq=4, debug=False, ncores=NCORES, trunc=None):
    nc = bass.Bass("TRN2", target_bir_lowering=False)
    di = {}

    def inp(name, shape):
        di[name] = nc.dram_tensor(name, list(shape), F32, kind="ExternalInput").ap()

    inp("x", (nseq, S, D))
    inp("qkv_w", (D, 3 * D)); inp("qkv_b_pm", (128, 24)); inp("qkv_b_row", (1, 3 * D))
    inp("rq_w", (D, 256)); inp("rq_b_pm", (128, 2))
    inp("rkv_w", (D, 512)); inp("rkv_b_pm", (128, 2)); inp("rkv_b_rv_row", (1, 256))
    inp("rout_w", (256, D)); inp("rout_b_pm", (128, 8))
    inp("out_w", (D, D)); inp("out_b_row", (1, D))
    inp("w1t", (D, H1)); inp("b1_pm", (128, 32))
    inp("w2t", (H1, D)); inp("b2_row", (1, D))
    inp("ident", (128, 128)); inp("masks", (128, 512)); inp("masks0", (128, 512))
    inp("latmask", (64, NB * 256)); inp("onescol", (128, 256))
    inp("pairsel", (16, 1024)); inp("latpairsel", (4, 256)); inp("onesrow", (1, 128))
    inp("cc", (128, 8)); inp("glpat", (128, 256)); inp("grpat", (128, 256))

    out_d = nc.dram_tensor("out", [nseq, S, D], F32, kind="ExternalOutput").ap()
    kd = "ExternalOutput" if debug else "Internal"
    xt_spill = nc.dram_tensor("xt_spill", [nseq, 8, 128, S], BF16).ap()
    y_spill = nc.dram_tensor("y_spill", [nseq, S, D], F32, kind=kd).ap()
    lny_spill = nc.dram_tensor("lny_spill", [nseq, S, D], F32, kind=kd).ap()
    h_spill = nc.dram_tensor("h_spill", [32, 128, nseq * S], F32, kind=kd).ap()
    cc1_in = nc.dram_tensor("cc1_in", [1, 1], F32).ap()
    cc1_out = nc.dram_tensor("cc1_out", [1, 1], F32).ap()
    cc2_in = nc.dram_tensor("cc2_in", [1, 1], F32).ap()
    cc2_out = nc.dram_tensor("cc2_out", [1, 1], F32).ap()

    NTOK = nseq * S

    with contextlib.ExitStack() as stack:
        E = stack.enter_context
        p = Prog(nc, stack)

        def sb(name, shape, dt=BF16):
            return E(nc.sbuf_tensor(name, list(shape), dt))

        ident16 = sb("ident16", (128, 128)); identf = sb("identf", (128, 128), F32)
        onescol16 = sb("onescol16", (128, 256))
        pairsel16 = sb("pairsel16", (16, 1024)); latpairsel16 = sb("latpairsel16", (4, 256))
        onesrow16 = sb("onesrow16", (1, 128))
        glpat16 = sb("glpat16", (128, 256)); grpat16 = sb("grpat16", (128, 256))
        cc_sb = sb("cc_sb", (128, 8), F32)
        qkv_b_pm = sb("qkv_b_pm_s", (128, 24), F32)
        qkv_b_vrow = sb("qkv_b_vrow", (1, 1024))
        rq_b_pm = sb("rq_b_pm_s", (128, 2), F32)
        rkv_b_pm = sb("rkv_b_pm_s", (128, 2), F32)
        rkv_b_rv_row = sb("rkv_b_rv_row_s", (1, 256), F32)
        rkv_b_rv32 = sb("rkv_b_rv32", (1, 256))
        rout_b_pm = sb("rout_b_pm_s", (128, 8), F32)
        out_b_row16 = sb("out_b_row16", (1, D))
        b1_pm = sb("b1_pm_s", (128, 32), F32); b1s = sb("b1s", (128, 32), F32)
        b2_row16 = sb("b2_row16", (1, D))
        x3 = [sb(f"x3_{i}", (128, D), F32) for i in range(2)]
        y_sb = [sb(f"y_sb_{i}", (128, D), F32) for i in range(2)]
        lny_sb = [sb(f"lny_sb_{i}", (128, D), F32) for i in range(2)]
        st = [sb(f"st_{i}", (128, 1), F32) for i in range(12)]
        lnmax = sb("lnmax", (128, 1), F32); hmax = sb("hmax", (128, 1), F32)
        w1max = sb("w1max", (128, 1), F32); w2max = sb("w2max", (128, 1), F32)
        rloc = sb("rloc", (16, 256), F32)
        gsig = sb("gsig", (16, 256), F32); gtmp = sb("gtmp", (16, 256), F32)
        apack = sb("apack", (16, 256)); bpack = sb("bpack", (16, 256))
        rlat16 = sb("rlat16", (4, 256)); rlatf = sb("rlatf", (4, 256), F32)
        scal = sb("scal", (1, 8), F32); wscal = sb("wscal", (1, 8), F32)
        barcell = sb("barcell", (1, 1), F32)
        onesf_row = sb("onesf_row", (1, 128), F32)
        rs_b = sb("rs_b", (128, 1), F32); rsh_b = sb("rsh_b", (128, 1), F32)
        ss1_b = sb("ss1_b", (128, 1), F32); ss2_b = sb("ss2_b", (128, 1), F32)
        rs1_b = sb("rs1_b", (128, 1), F32); rs2_b = sb("rs2_b", (128, 1), F32)

        abase = (nc.sbuf_base + 31) & ~31
        ARENA = 229344 - abase

        def sba(name, shape, dt, off):
            assert off + shape[1] * mybir.dt.size(dt) <= ARENA, name
            return nc.alloc_sbuf_tensor_at(name, list(shape), dt, offset=abase + off)

        o = 0
        masks16 = sba("masks16", (128, 512), BF16, o); o += 1024
        masks016 = sba("masks016", (128, 512), BF16, o); o += 1024
        latmask16 = sba("latmask16", (64, NB * 256), BF16, o); o += 4 * 1024
        qkv_w_sb = sba("qkv_w_sb", (128, 8 * 3072), BF16, o); o += 48 * 1024
        out_w_sb = sba("out_w_sb", (128, 8 * 1024), BF16, o); o += 16 * 1024
        rkv_w_sb = sba("rkv_w_sb", (128, 8 * 512), BF16, o); o += 8 * 1024
        rq_w_sb = sba("rq_w_sb", (128, 8 * 256), BF16, o); o += 4 * 1024
        rout_w_sb = sba("rout_w_sb", (128, 2 * 1024), BF16, o); o += 4 * 1024
        rqT = sba("rqT", (128, 2 * S), BF16, o); o += 8 * 1024
        pooledT = sba("pooledT", (128, 8 * 64), BF16, o); o += 1024
        rkT = sba("rkT", (128, 2 * 64), BF16, o); o += 256
        rv_sb = sba("rv_sb", (64, 256), BF16, o); o += 512
        kt_ring = sba("kt_ring", (128, 8 * 512), BF16, o); o += 8 * 1024
        v_ring = sba("v_ring", (128, 4 * 1024), BF16, o); o += 8 * 1024
        TRANS0 = o
        qt_sb = [sba(f"qt_{i}", (128, 8 * 256), BF16, o + i * 4096) for i in range(2)]; o += 8 * 1024
        pexp = [sba(f"pexp_{i}", (128, 512), BF16, o + i * 1024) for i in range(2)]; o += 2 * 1024
        pmask = [sba(f"pmask_{i}", (128, 512), BF16, o + i * 1024) for i in range(2)]; o += 2 * 1024
        localT = [sba(f"localT_{i}", (128, 8 * 256), BF16, o + i * 4096) for i in range(2)]; o += 8 * 1024
        remoteT = [sba(f"remoteT_{i}", (128, 8 * 256), BF16, o + i * 4096) for i in range(2)]; o += 8 * 1024
        mixedT = [sba(f"mixedT_{i}", (128, 8 * 256), BF16, o + i * 4096) for i in range(2)]; o += 8 * 1024
        latn = [sba(f"latn_{i}", (128, 512), BF16, o + i * 1024) for i in range(2)]; o += 2 * 1024
        platm = [sba(f"platm_{i}", (64, 1024), BF16, o + i * 2048) for i in range(2)]; o += 4 * 1024
        mxt = [sba(f"mxt_{i}", (128, 256), BF16, o + i * 512) for i in range(2)]; o += 1024
        xtb = [sba(f"xtb_{i}", (128, 8 * 256), BF16, o + i * 4096) for i in range(2)]; o += 8 * 1024
        o1 = TRANS0
        x1 = [sba(f"x1_{i}", (128, D), F32, o1 + i * 4096) for i in range(2)]; o1 += 8 * 1024
        lnx = [sba(f"lnx_{i}", (128, D), BF16, o1 + i * 2048) for i in range(2)]; o1 += 4 * 1024
        xtg = [sba(f"xtg_{i}", (128, 8 * 512), BF16, o1 + i * 8192) for i in range(2)]; o1 += 16 * 1024
        pooledf = sba("pooledf", (128, 512), F32, o1); o1 += 2 * 1024
        assert o1 <= o <= ARENA
        o2 = 0
        wq_sb = sba("wq_sb", (128, 32768), BF16, o2); o2 += 64 * 1024
        wstage = [sba(f"wstage_{i}", (128, 2048), F32, o2 + i * 8192) for i in range(2)]; o2 += 16 * 1024
        hq_sb2 = [sba(f"hq_sb_{i}", (128, 32 * 256), BF16, o2 + i * 16 * 1024) for i in range(2)]; o2 += 32 * 1024
        qtile = [sba(f"qtile_{i}", (128, D), F32, o2 + i * 4096) for i in range(2)]; o2 += 8 * 1024
        xqT = [sba(f"xqT_{i}", (128, 8 * 512), BF16, o2 + i * 8192) for i in range(2)]; o2 += 16 * 1024
        hT_st = [sba(f"hT_st_{i}", (128, 512), F32, o2 + i * 2048) for i in range(2)]; o2 += 4 * 1024
        hload = [sba(f"hload_{i}", (128, 512), F32, o2 + i * 2048) for i in range(2)]; o2 += 4 * 1024
        hldt = sba("hldt", (128, 256), F32, o2); o2 += 1024
        oout = [sba(f"oout_{i}", (128, D), F32, o2 + i * 4096) for i in range(2)]; o2 += 8 * 1024
        assert o2 <= ARENA

        mm = [E(nc.psum_tensor(f"mm_{i}", [128, 512], F32)) for i in range(3)]
        pk = [E(nc.psum_tensor(f"pk_{i}", [128, 512], F32)) for i in range(2)]
        out2 = E(nc.psum_tensor("out2", [128, 1024], F32))
        trb = E(nc.psum_tensor("trb", [128, 1024], BF16))
        MMR = ["mm0", "mm1", "mm2"]
        PKR = ["pk0", "pk1"]

        p.ALIASES.update({
            "x1_0": ["qt_0"], "qt_0": ["x1_0"],
            "x1_1": ["qt_1"], "qt_1": ["x1_1"],
            "lnx_0": ["pexp_0", "pexp_1"], "pexp_0": ["lnx_0"], "pexp_1": ["lnx_0"],
            "lnx_1": ["pmask_0", "pmask_1"], "pmask_0": ["lnx_1"], "pmask_1": ["lnx_1"],
            "xtg_0": ["localT_0", "localT_1"], "localT_0": ["xtg_0"], "localT_1": ["xtg_0"],
            "xtg_1": ["remoteT_0", "remoteT_1"], "remoteT_0": ["xtg_1"], "remoteT_1": ["xtg_1"],
            "pooledf": ["mixedT_0"], "mixedT_0": ["pooledf"],
        })
        for en in Prog.ENGINES:
            p.esem[en] = p.sem("e_" + en)
        sX = p.esem["gpsimd"]
        dsems = {}
        for n_ in ("dW", "dCst", "dX0", "dX1", "dA0", "dA1", "dB0", "dB1",
                   "dY0", "dY1", "dQ0", "dQ1", "dH0", "dH1", "dH2", "dH3", "dWq0", "dWq1",
                   "xS0", "xS1", "ySp0", "ySp1", "lSp0", "lSp1",
                   "hS0", "hS1", "oS0", "oS1", "dXT", "dSP", "dHS", "dO", "dCC"):
            dsems[n_] = p.sem(n_)

        def dma(queue, out, in_, semname, reads=(), writes=(), waits=()):
            p.op(queue, lambda e, o=out, i=in_: e.dma_start(out=o, in_=i),
                 reads=reads, writes=writes, waits=waits, sem=dsems[semname], amt=16)

        def aft(sem):
            return (sem, sem.n)

        def bcast_scalar(src_ap, src_res, dst, dst_res):
            p.op("tensor", lambda e, src_ap=src_ap: e.matmul(mm[0][:, 0:1], onesf_row[:], src_ap, start=True, stop=True),
                 reads=[src_res, "onesf_row"], writes=[MMR[0]])
            p.op("vector", lambda e, dst=dst: e.tensor_copy(dst[:], mm[0][:, 0:1]),
                 reads=[MMR[0]], writes=[dst_res])

        def bar():
            w = [aft(s_) for s_ in dsems.values()] + [aft(p.esem[e]) for e in Prog.ENGINES if e != "vector"]
            p.op("vector", lambda e: e.memset(barcell[:], 0.0), waits=w, writes=["barcell"])
            return aft(p.esem["vector"])

        for name, dst in [("ident", ident16), ("masks", masks16), ("masks0", masks016),
                          ("latmask", latmask16), ("onescol", onescol16), ("pairsel", pairsel16),
                          ("latpairsel", latpairsel16), ("onesrow", onesrow16),
                          ("glpat", glpat16), ("grpat", grpat16), ("b2_row", b2_row16),
                          ("out_b_row", out_b_row16), ("rkv_b_rv_row", rkv_b_rv_row)]:
            dma("gpsimd", dst[:], di[name], "dW", writes=[dst.name])
        dma("gpsimd", qkv_b_vrow[:], di["qkv_b_row"][0:1, 2048:3072], "dW", writes=["qkv_b_vrow"])
        for name, dst in [("ident", identf), ("cc", cc_sb), ("qkv_b_pm", qkv_b_pm),
                          ("rq_b_pm", rq_b_pm), ("rkv_b_pm", rkv_b_pm), ("rout_b_pm", rout_b_pm),
                          ("b1_pm", b1_pm)]:
            dma("sync", dst[:], di[name], "dCst", writes=[dst.name if dst.name else name])
        for k in range(8):
            dma("gpsimd", qkv_w_sb[:, k * 3072:(k + 1) * 3072], di["qkv_w"][k * 128:(k + 1) * 128, :], "dW", writes=["qkv_w"])
            dma("gpsimd", out_w_sb[:, k * 1024:(k + 1) * 1024], di["out_w"][k * 128:(k + 1) * 128, :], "dW", writes=["out_w"])
            dma("gpsimd", rkv_w_sb[:, k * 512:(k + 1) * 512], di["rkv_w"][k * 128:(k + 1) * 128, :], "dW", writes=["rkv_w"])
            dma("gpsimd", rq_w_sb[:, k * 256:(k + 1) * 256], di["rq_w"][k * 128:(k + 1) * 128, :], "dW", writes=["rq_w"])
        for k in range(2):
            dma("gpsimd", rout_w_sb[:, k * 1024:(k + 1) * 1024], di["rout_w"][k * 128:(k + 1) * 128, :], "dW", writes=["rout_w"])
        for rr_ in p.res.values():
            if rr_.w is not None and rr_.w[0] in (dsems["dW"], dsems["dCst"]):
                rr_.w = (rr_.w[0], rr_.w[0].n)
        p.op("vector", lambda e: e.tensor_scalar(rkv_b_rv32[:], rkv_b_rv_row[:], 32.0, None, ALU.mult),
             reads=["rkv_b_rv_row_s"], writes=["rkv_b_rv32"])
        p.op("vector", lambda e: e.memset(onesf_row[:], 1.0), writes=["onesf_row"])
        p.op("vector", lambda e: e.memset(kt_ring[:], 0.0), writes=["ktr0", "ktr1", "ktr2", "ktr3"])
        p.op("vector", lambda e: e.memset(v_ring[:], 0.0), writes=["vr0", "vr1", "vr2", "vr3"])

        def wabs_stream(wname, nrt, ncol, maxres, maxbuf):
            first = True
            idx = 0
            for kt in range(nrt):
                for ci in range(ncol // 1024):
                    buf = y_sb[idx % 2]
                    bn = f"y_sb_{idx % 2}"
                    dma("sync", buf[:], di[wname][kt * 128:(kt + 1) * 128, ci * 1024:(ci + 1) * 1024],
                        "dY%d" % (idx % 2), writes=[bn])
                    if first:
                        p.op("vector", lambda e, b=buf: e.tensor_reduce(maxbuf[:], b[:], AX.X, ALU.max, apply_absolute_value=True),
                             reads=[bn], writes=[maxres])
                        first = False
                    else:
                        p.op("vector", lambda e, b=buf: e.tensor_reduce(st[0][:], b[:], AX.X, ALU.max, apply_absolute_value=True),
                             reads=[bn], writes=["st0"])
                        p.op("vector", lambda e: e.tensor_max(maxbuf[:], maxbuf[:], st[0][:]),
                             reads=["st0"], writes=[maxres])
                    idx += 1
        wabs_stream("w1t", 8, H1, "w1max", w1max)
        wabs_stream("w2t", 32, D, "w2max", w2max)
        p.op("tensor", lambda e: e.transpose(mm[0][0:1, 0:128], w1max[:], identf[:]),
             reads=["w1max", "identf"], writes=["mm0"])
        p.op("vector", lambda e: e.reduce_max(wscal[0:1, 0:1], mm[0][0:1, 0:128], AX.X),
             reads=["mm0"], writes=["wscal"])
        p.op("tensor", lambda e: e.transpose(mm[0][0:1, 128:256], w2max[:], identf[:]),
             reads=["w2max", "identf"], writes=["mm0"])
        p.op("vector", lambda e: e.reduce_max(wscal[0:1, 3:4], mm[0][0:1, 128:256], AX.X),
             reads=["mm0"], writes=["wscal"])
        p.op("vector", lambda e: e.tensor_scalar(wscal[0:1, 1:2], wscal[0:1, 0:1], 1e-6, 1.0 / 127.0, ALU.max, ALU.mult), reads=["wscal"], writes=["wscal"])
        p.op("vector", lambda e: e.reciprocal(wscal[0:1, 2:3], wscal[0:1, 1:2]), reads=["wscal"], writes=["wscal"])
        p.op("vector", lambda e: e.tensor_scalar(wscal[0:1, 4:5], wscal[0:1, 3:4], 1e-6, 1.0 / 127.0, ALU.max, ALU.mult), reads=["wscal"], writes=["wscal"])
        p.op("vector", lambda e: e.reciprocal(wscal[0:1, 5:6], wscal[0:1, 4:5]), reads=["wscal"], writes=["wscal"])
        bcast_scalar(wscal[0:1, 2:3], "wscal", rs1_b, "rs1_b")
        bcast_scalar(wscal[0:1, 5:6], "wscal", rs2_b, "rs2_b")

        def ln_stats(src_name, src, junk, par=0):
            o6 = par * 6
            sB = st[o6:o6 + 6]
            N = [f"st{o6 + i}" for i in range(6)]
            p.op("vector", lambda e, sB=sB: e.reduce_sum(sB[0][:], src[:], AX.X), reads=[src_name], writes=[N[0]])
            p.op("scalar", lambda e, sB=sB: e.activation(junk[0][:], src[:], AF.Square, accum_out=sB[1][:]),
                 reads=[src_name], writes=[junk[1], N[1]])
            p.op("vector", lambda e, sB=sB: e.tensor_scalar(sB[2][:], sB[0][:], 1.0 / 1024.0, None, ALU.mult), reads=[N[0]], writes=[N[2]])
            p.op("vector", lambda e, sB=sB: e.tensor_scalar(sB[3][:], sB[1][:], 1.0 / 1024.0, None, ALU.mult), reads=[N[1]], writes=[N[3]])
            p.op("vector", lambda e, sB=sB: e.tensor_mul(sB[4][:], sB[2][:], sB[2][:]), reads=[N[2]], writes=[N[4]])
            p.op("vector", lambda e, sB=sB: e.tensor_sub(sB[3][:], sB[3][:], sB[4][:]), reads=[N[3], N[4]], writes=[N[3]])
            p.op("scalar", lambda e, sB=sB: e.activation(sB[4][:], sB[3][:], AF.Sqrt, bias=cc_sb[:, CC_EPS:CC_EPS + 1]),
                 reads=[N[3], "cc_sb"], writes=[N[4]])
            p.op("vector", lambda e, sB=sB: e.reciprocal(sB[5][:], sB[4][:]), reads=[N[4]], writes=[N[5]])
            p.op("vector", lambda e, sB=sB: e.tensor_scalar(sB[2][:], sB[2][:], sB[5][:], -1.0, ALU.mult, ALU.mult),
                 reads=[N[2], N[5]], writes=[N[2]])
            return sB, N

        nc._mark_p0 = len(p.ops)
        lnmax_init = [False]
        for q in range(nseq):
            for g in range(4):
                gx = xtg[g % 2]
                gxn = f"xtg_{g % 2}"
                for t in range(4):
                    T = g * 4 + t
                    xb, xbn = x1[T % 2], f"x1_{T % 2}"
                    dma("sync", xb[:], di["x"][q, T * 128:(T + 1) * 128, :], "dX%d" % (T % 2), writes=[xbn])
                    sB, N = ln_stats(xbn, xb, (lny_sb[T % 2], f"lny_sb_{T % 2}"), par=T % 2)
                    lb, lbn = lnx[T % 2], f"lnx_{T % 2}"
                    p.op("scalar", lambda e, xb=xb, lb=lb, sB=sB: e.activation(lb[:], xb[:], AF.Identity, bias=sB[2][:], scale=sB[5][:]),
                         reads=[xbn, N[2], N[5]], writes=[lbn])

                    def mk_tr(lb=lb):
                        def f(e):
                            ins = None
                            for k in range(8):
                                ins = e.transpose(trb[:, k * 128:(k + 1) * 128], lb[:, k * 128:(k + 1) * 128], ident16[:])
                            return ins
                        return f
                    p.op("tensor", mk_tr(), reads=[lbn, "ident16"], writes=["trb"])
                    p.op("vector", lambda e, gx=gx, t=t: e.tensor_copy(
                        gx[:].rearrange("p (k s) -> p k s", k=8)[:, :, t * 128:(t + 1) * 128],
                        trb[:].rearrange("p (k s) -> p k s", k=8)),
                        reads=["trb"], writes=[gxn])
                for pr in range(2):
                    bank = MMR[2]

                    def mk_rq(pr=pr, gx=gx):
                        def f(e):
                            ins = None
                            for k in range(8):
                                ins = e.matmul(mm[2][:, 0:512], rq_w_sb[:, k * 256 + pr * 128:k * 256 + (pr + 1) * 128],
                                               gx[:, k * 512:(k + 1) * 512], start=(k == 0), stop=(k == 7))
                            return ins
                        return f
                    p.op("tensor", mk_rq(), reads=[gxn, "rq_w"], writes=[bank])
                    p.op("scalar", lambda e, pr=pr, g=g: e.activation(
                        rqT[:, pr * S + g * 512:pr * S + (g + 1) * 512], mm[2][:, 0:512], AF.Identity,
                        bias=rq_b_pm[:, pr:pr + 1]), reads=[bank, "rq_b_pm_s"], writes=["rqT"])

                def mk_pool(gx=gx, g=g):
                    def f(e):
                        ins = None
                        for k in range(8):
                            ins = e.reduce_sum(pooledf[:, k * 64 + g * 16:k * 64 + (g + 1) * 16],
                                               gx[:, k * 512:(k + 1) * 512].rearrange("p (l w) -> p l w", w=32), AX.X)
                        return ins
                    return f
                p.op("vector", mk_pool(), reads=[gxn], writes=["pooledf"])
                for k in range(8):
                    dma("sync", xt_spill[q, k, :, g * 512:(g + 1) * 512], gx[:, k * 512:(k + 1) * 512],
                        "xS%d" % (g % 2), reads=[gxn])
            XT_READY0 = aft(dsems["xS0"])
            XT_READY1 = aft(dsems["xS1"])

            p.op("vector", lambda e: e.tensor_copy(pooledT[:], pooledf[:]), reads=["pooledf"], writes=["pooledT"])
            for pr in range(2):
                def mk_rk(pr=pr):
                    def f(e):
                        ins = None
                        for k in range(8):
                            ins = e.matmul(mm[0][:, 0:64], rkv_w_sb[:, k * 512 + pr * 128:k * 512 + (pr + 1) * 128],
                                           pooledT[:, k * 64:(k + 1) * 64], start=(k == 0), stop=(k == 7))
                        return ins
                    return f
                p.op("tensor", mk_rk(), reads=["pooledT", "rkv_w"], writes=[MMR[0]])
                p.op("scalar", lambda e, pr=pr: e.activation(
                    rkT[:, pr * 64:(pr + 1) * 64], mm[0][:, 0:64], AF.Identity,
                    bias=rkv_b_pm[:, pr:pr + 1], scale=cc_sb[:, CC_1_32:CC_1_32 + 1]),
                    reads=[MMR[0], "rkv_b_pm_s", "cc_sb"], writes=["rkT"])

            def mk_rv(e):
                ins = None
                for k in range(8):
                    ins = e.matmul(mm[1][0:64, 0:256], pooledT[:, k * 64:(k + 1) * 64],
                                   rkv_w_sb[:, k * 512 + 256:k * 512 + 512], start=(k == 0), stop=False)
                ins = e.matmul(mm[1][0:64, 0:256], onesrow16[0:1, 0:64], rkv_b_rv32[:], start=False, stop=True)
                return ins
            p.op("tensor", mk_rv, reads=["pooledT", "rkv_w", "onesrow16", "rkv_b_rv32"], writes=[MMR[1]])
            p.op("scalar", lambda e: e.activation(rv_sb[:, :], mm[1][0:64, 0:256], AF.Identity,
                                                  scale=cc_sb[0:64, CC_1_32:CC_1_32 + 1]),
                 reads=[MMR[1], "cc_sb"], writes=["rv_sb"])

            for b in range(NB):
                bb = b % 2
                pkb = PKR[bb]
                msk, mskn = (masks016, "masks016") if b == 0 else (masks16, "masks16")
                for k in range(8):
                    dma("sync", xtb[bb][:, k * 256:(k + 1) * 256], xt_spill[q, k, :, b * 256:(b + 1) * 256],
                        "dB%d" % bb, writes=[f"xtb_{bb}"] if k == 0 else (), reads=() if k else (),
                        waits=[XT_READY0, XT_READY1])
                p.r(f"xtb_{bb}").w = (dsems["dB%d" % bb], dsems["dB%d" % bb].n)
                for t in range(2):
                    dma("sync", x3[t][:], di["x"][q, b * 256 + t * 128:b * 256 + (t + 1) * 128, :],
                        "dA%d" % t, writes=[f"x3_{t}"])

                pack_ops = []

                for lh in range(4):
                    pr, par = lh // 2, lh % 2
                    bank = MMR[lh % 2]

                    def mk_lat(pr=pr, par=par, b=b, lh=lh):
                        def f(e):
                            return e.matmul(mm[lh % 2][0:64, 0:256],
                                            rkT[par * 64:(par + 1) * 64, pr * 64:(pr + 1) * 64],
                                            rqT[par * 64:(par + 1) * 64, pr * S + b * 256:pr * S + (b + 1) * 256],
                                            start=True, stop=True)
                        return f
                    p.op("tensor", mk_lat(), reads=["rkT", "rqT"], writes=[bank])
                    p.op("scalar", lambda e, lh=lh, bb=bb: e.activation(
                        platm[bb][0:64, lh * 256:(lh + 1) * 256], mm[lh % 2][0:64, 0:256],
                        AF.Exp, scale=cc_sb[0:64, CC_SCL:CC_SCL + 1]),
                        reads=[bank, "cc_sb"], writes=[f"platm_{bb}"] if lh == 0 else ())
                    if lh > 0:
                        p.r(f"platm_{bb}").w = (p.esem["scalar"], p.esem["scalar"].n)
                def mk_lmask(bb=bb, b=b):
                    def f(e):
                        ins = None
                        for lh in range(4):
                            ins = e.tensor_mul(platm[bb][0:64, lh * 256:(lh + 1) * 256],
                                               platm[bb][0:64, lh * 256:(lh + 1) * 256],
                                               latmask16[:, b * 256:(b + 1) * 256])
                        return ins
                    return f
                p.op("vector", mk_lmask(), reads=[f"platm_{bb}", "latmask16"], writes=[f"platm_{bb}"])
                for lh in range(4):
                    pr, par = lh // 2, lh % 2

                    def mk_latden(lh=lh, bb=bb, first=(lh == 0)):
                        def f(e):
                            return e.matmul(pk[bb][0:16, 256:512], onescol16[0:64, lh * 16:(lh + 1) * 16],
                                            platm[bb][0:64, lh * 256:(lh + 1) * 256],
                                            start=first, stop=(lh == 3))
                        return f
                    p.op("tensor", mk_latden(), reads=[f"platm_{bb}", "onescol16"],
                         writes=[pkb])
                def mk_latpv(bb=bb):
                    def f(e):
                        ins = None
                        for lh in range(4):
                            pr, par = lh // 2, lh % 2
                            ins = e.matmul(mm[2][par * 64:(par + 1) * 64, pr * 256:(pr + 1) * 256],
                                           rv_sb[:, lh * 64:(lh + 1) * 64],
                                           platm[bb][0:64, lh * 256:(lh + 1) * 256],
                                           start=True, stop=True)
                        return ins
                    return f
                p.op("tensor", mk_latpv(), reads=[f"platm_{bb}", "rv_sb"], writes=[MMR[2]])
                p.op("vector", lambda e, bb=bb: e.tensor_scalar(rlatf[:], pk[bb][0:4, 256:512], 1e-30, None, ALU.max),
                     reads=[pkb], writes=["rlatf"])
                p.op("vector", lambda e: e.reciprocal(rlatf[:], rlatf[:]), reads=["rlatf"], writes=["rlatf"])
                p.op("vector", lambda e: e.tensor_copy(rlat16[:], rlatf[:]), reads=["rlatf"], writes=["rlat16"])
                def mk_latbc(e):
                    i1 = e.matmul(mm[0][:, 0:256], latpairsel16[:, 0:128], rlat16[:], start=True, stop=True)
                    i2 = e.matmul(mm[0][:, 256:512], latpairsel16[:, 128:256], rlat16[:], start=True, stop=True)
                    return i2
                p.op("tensor", mk_latbc, reads=["rlat16", "latpairsel16"], writes=[MMR[0]])
                p.op("scalar", lambda e, bb=bb: e.activation(pexp[bb][:], mm[2][:, 0:512], AF.Identity),
                     reads=[MMR[2]], writes=[f"pexp_{bb}"])
                for pr in range(2):
                    p.op("vector", lambda e, pr=pr, bb=bb: e.tensor_mul(
                        latn[bb][:, pr * 256:(pr + 1) * 256], pexp[bb][:, pr * 256:(pr + 1) * 256], mm[0][:, pr * 256:(pr + 1) * 256]),
                        reads=[f"pexp_{bb}", MMR[0]], writes=[f"latn_{bb}"] if pr == 0 else ())
                p.r(f"latn_{bb}").w = (p.esem["vector"], p.esem["vector"].n)
                for pr in range(8):
                    for wi, base in ((0, 0), (1, 1024)):
                        bidx = 1 + (2 * pr + wi) % 2
                        bank = MMR[bidx]

                        def mk_qk(pr=pr, base=base, bb=bb, bidx=bidx):
                            def f(e):
                                ins = None
                                for k in range(8):
                                    ins = e.matmul(mm[bidx][:, 0:256],
                                                   qkv_w_sb[:, k * 3072 + base + pr * 128:k * 3072 + base + (pr + 1) * 128],
                                                   xtb[bb][:, k * 256:(k + 1) * 256], start=(k == 0), stop=(k == 7))
                                return ins
                            return f
                        p.op("tensor", mk_qk(), reads=[f"xtb_{bb}", "qkv_w"], writes=[bank])
                        if wi == 0:
                            p.op("scalar", lambda e, pr=pr, bb=bb, bidx=bidx: e.activation(
                                qt_sb[bb][:, pr * 256:(pr + 1) * 256], mm[bidx][:, 0:256], AF.Identity,
                                bias=qkv_b_pm[:, pr:pr + 1]), reads=[bank, "qkv_b_pm_s"],
                                writes=[f"qt_{bb}"] if pr == 0 else ())
                            if pr > 0:
                                p.r(f"qt_{bb}").w = (p.esem["scalar"], p.esem["scalar"].n)
                        else:
                            s0, s1 = 2 * b % 4, (2 * b + 1) % 4

                            def mk_kev(pr=pr, s0=s0, s1=s1, bidx=bidx):
                                def f(e):
                                    e.activation(kt_ring[:, pr * 512 + s0 * 128:pr * 512 + s0 * 128 + 128],
                                                 mm[bidx][:, 0:128], AF.Identity, bias=qkv_b_pm[:, 8 + pr:9 + pr])
                                    return e.activation(kt_ring[:, pr * 512 + s1 * 128:pr * 512 + s1 * 128 + 128],
                                                        mm[bidx][:, 128:256], AF.Identity, bias=qkv_b_pm[:, 8 + pr:9 + pr])
                                return f
                            p.op("scalar", mk_kev(), reads=[bank, "qkv_b_pm_s"],
                                 writes=[f"ktr{s0}", f"ktr{s1}"] if pr == 0 else ())
                            if pr > 0:
                                ident_ = (p.esem["scalar"], p.esem["scalar"].n)
                                p.r(f"ktr{s0}").w = ident_
                                p.r(f"ktr{s1}").w = ident_
                for t in range(2):
                    slot = (2 * b + t) % 4

                    def mk_v(t=t, bb=bb):
                        def f(e):
                            ins = None
                            for k in range(8):
                                for half in range(2):
                                    ins = e.matmul(out2[:, half * 512:(half + 1) * 512],
                                                   xtb[bb][:, k * 256 + t * 128:k * 256 + (t + 1) * 128],
                                                   qkv_w_sb[:, k * 3072 + 2048 + half * 512:k * 3072 + 2048 + (half + 1) * 512],
                                                   start=(k == 0), stop=False)
                            ins = e.matmul(out2[:, 0:512], onesrow16[:], qkv_b_vrow[0:1, 0:512], start=False, stop=True)
                            ins = e.matmul(out2[:, 512:1024], onesrow16[:], qkv_b_vrow[0:1, 512:1024], start=False, stop=True)
                            return ins
                        return f
                    p.op("tensor", mk_v(), reads=[f"xtb_{bb}", "qkv_w", "onesrow16", "qkv_b_vrow"], writes=["out2", "out2a", "out2b"])
                    p.op("vector", lambda e, slot=slot: e.tensor_copy(v_ring[:, slot * 1024:(slot + 1) * 1024], out2[:]),
                         reads=["out2", "out2a", "out2b"], writes=[f"vr{slot}"])

                for ds in range(8):
                    bank = MMR[1 + ds % 2]

                    def mk_rout(ds=ds, bb=bb, bank=1 + ds % 2):
                        def f(e):
                            ins = None
                            for k in range(2):
                                ins = e.matmul(mm[bank][:, 0:256], rout_w_sb[:, k * 1024 + ds * 128:k * 1024 + (ds + 1) * 128],
                                               latn[bb][:, k * 256:(k + 1) * 256], start=(k == 0), stop=(k == 1))
                            return ins
                        return f
                    p.op("tensor", mk_rout(), reads=[f"latn_{bb}", "rout_w"], writes=[MMR[1 + ds % 2]])
                    p.op("scalar", lambda e, ds=ds, bb=bb, bank=1 + ds % 2: e.activation(
                        remoteT[bb][:, ds * 256:(ds + 1) * 256], mm[bank][:, 0:256], AF.Identity,
                        bias=rout_b_pm[:, ds:ds + 1]), reads=[MMR[1 + ds % 2], "rout_b_pm_s"],
                        writes=[f"remoteT_{bb}"] if ds == 0 else ())
                p.r(f"remoteT_{bb}").w = (p.esem["scalar"], p.esem["scalar"].n)

                sA, sB, sC_ = (2 * b - 1) % 4, 2 * b % 4, (2 * b + 1) % 4
                for h in range(16):
                    pr, par = h // 2, h % 2
                    bank = MMR[h % 3]
                    pxn, pmn = f"pexp_{h % 2}", f"pmask_{h % 2}"

                    def mk_sc(pr=pr, par=par, bb=bb, bidx=h % 3, sA=sA, sB=sB, sC_=sC_):
                        def f(e):
                            e.matmul(mm[bidx][:, 0:128],
                                     kt_ring[par * 64:(par + 1) * 64, pr * 512 + sA * 128:pr * 512 + (sA + 1) * 128],
                                     qt_sb[bb][par * 64:(par + 1) * 64, pr * 256:pr * 256 + 128], start=True, stop=True)
                            e.matmul(mm[bidx][:, 128:384],
                                     kt_ring[par * 64:(par + 1) * 64, pr * 512 + sB * 128:pr * 512 + (sB + 1) * 128],
                                     qt_sb[bb][par * 64:(par + 1) * 64, pr * 256:(pr + 1) * 256], start=True, stop=True)
                            return e.matmul(mm[bidx][:, 384:512],
                                            kt_ring[par * 64:(par + 1) * 64, pr * 512 + sC_ * 128:pr * 512 + (sC_ + 1) * 128],
                                            qt_sb[bb][par * 64:(par + 1) * 64, pr * 256 + 128:(pr + 1) * 256], start=True, stop=True)
                        return f
                    p.op("tensor", mk_sc(), reads=[f"ktr{sA}", f"ktr{sB}", f"ktr{sC_}", f"qt_{bb}"], writes=[bank])
                    p.op("scalar", lambda e, bidx=h % 3, pidx=h % 2: e.activation(pexp[pidx][:], mm[bidx][:], AF.Exp,
                                                                      scale=cc_sb[:, CC_SCL:CC_SCL + 1]),
                         reads=[bank, "cc_sb"], writes=[pxn])
                    p.op("vector", lambda e, pidx=h % 2, msk=msk: e.tensor_mul(pmask[pidx][:], pexp[pidx][:], msk[:]),
                         reads=[pxn, mskn], writes=[pmn])

                    def mk_den(h=h, bidx=h % 2, bb=bb):
                        def f(e):
                            ins = None
                            segs = [(0, 128, 0, 128), (128, 384, 0, 256), (384, 512, 128, 256)]
                            if h == 0:
                                segs = [segs[1], segs[0], segs[2]]
                            for si, (c0, c1, q0, q1) in enumerate(segs):
                                ins = e.matmul(pk[bb][0:16, q0:q1], onescol16[:, h * 16:(h + 1) * 16],
                                               pmask[bidx][:, c0:c1], start=(h == 0 and si == 0),
                                               stop=(h == 15 and c0 == 384))
                            return ins
                        return f
                    p.op("tensor", mk_den(), reads=[pmn, "onescol16"], writes=[pkb])
                    pvbank = ["out2a", "out2b"][h % 2]
                    pvoff = [0, 512][h % 2]

                    def mk_pv(h=h, par=par, pidx=h % 2, pvoff=pvoff, sA=sA, sB=sB, sC_=sC_):
                        def f(e):
                            segs = [(sB, 128, 384, 0, 256), (sA, 0, 128, 0, 128), (sC_, 384, 512, 128, 256)]
                            ins = None
                            fi = True
                            for (slot, c0, c1, q0, q1) in segs:
                                ins = e.matmul(out2[par * 64:(par + 1) * 64, pvoff + q0:pvoff + q1],
                                               v_ring[:, slot * 1024 + h * 64:slot * 1024 + (h + 1) * 64],
                                               pmask[pidx][:, c0:c1], start=fi, stop=(c0 == 384))
                                fi = False
                            return ins
                        return f
                    p.op("tensor", mk_pv(), reads=[pmn, f"vr{sA}", f"vr{sB}", f"vr{sC_}"], writes=[pvbank])
                    p.op("vector", lambda e, pr=pr, par=par, bb=bb, pvoff=pvoff: e.tensor_copy(
                        localT[bb][par * 64:(par + 1) * 64, pr * 256:(pr + 1) * 256], out2[par * 64:(par + 1) * 64, pvoff:pvoff + 256]),
                        reads=[pvbank], writes=[f"localT_{bb}"] if h == 0 else ())
                    if h > 0:
                        p.r(f"localT_{bb}").w = (p.esem["vector"], p.esem["vector"].n)
                    if par == 1:
                        for h2 in (2 * pr, 2 * pr + 1):
                            p.op("tensor", lambda e, h2=h2, pr=pr, bb=bb: e.matmul(
                                pk[bb][32:48, 0:256], glpat16[:, h2 * 16:(h2 + 1) * 16],
                                localT[bb][:, pr * 256:(pr + 1) * 256], start=(h2 == 0), stop=(h2 == 15)),
                                reads=[f"localT_{bb}", "glpat16"], writes=[pkb])
                        for h2 in (2 * pr, 2 * pr + 1):
                            last = (h2 == 15)
                            p.op("tensor", lambda e, h2=h2, pr=pr, bb=bb, last=last: e.matmul(
                                pk[bb][64:80, 0:256], grpat16[:, h2 * 16:(h2 + 1) * 16],
                                remoteT[bb][:, pr * 256:(pr + 1) * 256], start=(h2 == 0), stop=last),
                                reads=[f"remoteT_{bb}", "grpat16"], writes=[pkb])

                p.op("vector", lambda e, bb=bb: e.tensor_scalar(rloc[:], pk[bb][0:16, 0:256], 1e-30, None, ALU.max),
                     reads=[pkb], writes=["rloc"])
                p.op("vector", lambda e: e.reciprocal(rloc[:], rloc[:]), reads=["rloc"], writes=["rloc"])
                p.op("vector", lambda e, bb=bb: e.tensor_mul(gtmp[:], pk[bb][32:48, 0:256], rloc[:]),
                     reads=[pkb, "rloc"], writes=["gtmp"])
                p.op("vector", lambda e, bb=bb: e.tensor_add(gtmp[:], gtmp[:], pk[bb][64:80, 0:256]),
                     reads=["gtmp", pkb], writes=["gtmp"])
                p.op("scalar", lambda e: e.activation(gsig[:], gtmp[:], AF.Sigmoid), reads=["gtmp"], writes=["gsig"])
                p.op("vector", lambda e: e.tensor_mul(gtmp[:], gsig[:], rloc[:]), reads=["gsig", "rloc"], writes=["gtmp"])
                p.op("vector", lambda e: e.tensor_copy(apack[:], gtmp[:]), reads=["gtmp"], writes=["apack"])
                p.op("vector", lambda e: e.tensor_scalar(bpack[:], gsig[:], -1.0, 1.0, ALU.mult, ALU.add),
                     reads=["gsig"], writes=["bpack"])

                for pr in range(8):
                    bci = pr % 3
                    def mk_bc(pr=pr, bci=bci):
                        def f(e):
                            e.matmul(mm[bci][:, 0:256], pairsel16[:, pr * 128:(pr + 1) * 128], apack[:], start=True, stop=True)
                            return e.matmul(mm[bci][:, 256:512], pairsel16[:, pr * 128:(pr + 1) * 128], bpack[:], start=True, stop=True)
                        return f
                    p.op("tensor", mk_bc(), reads=["apack", "bpack", "pairsel16"], writes=[MMR[bci]])
                    mt = mxt[pr % 2]
                    p.op("vector", lambda e, pr=pr, bb=bb, mt=mt, bci=bci: e.tensor_mul(
                        mt[:], remoteT[bb][:, pr * 256:(pr + 1) * 256], mm[bci][:, 256:512]),
                        reads=[f"remoteT_{bb}", MMR[bci]], writes=[f"mxt_{pr % 2}"])
                    p.op("vector", lambda e, pr=pr, bb=bb, bci=bci: e.tensor_mul(
                        mixedT[bb][:, pr * 256:(pr + 1) * 256], localT[bb][:, pr * 256:(pr + 1) * 256], mm[bci][:, 0:256]),
                        reads=[f"localT_{bb}", MMR[bci]], writes=[f"mixedT_{bb}"] if pr == 0 else ())
                    p.op("vector", lambda e, pr=pr, bb=bb, mt=mt: e.tensor_add(
                        mixedT[bb][:, pr * 256:(pr + 1) * 256], mixedT[bb][:, pr * 256:(pr + 1) * 256], mt[:]),
                        reads=[f"mxt_{pr % 2}"], writes=())
                    p.r(f"mixedT_{bb}").w = (p.esem["vector"], p.esem["vector"].n)

                for t in range(2):
                    def mk_op(t=t, bb=bb):
                        def f(e):
                            ins = None
                            for k in range(8):
                                for half in range(2):
                                    ins = e.matmul(out2[:, half * 512:(half + 1) * 512],
                                                   mixedT[bb][:, k * 256 + t * 128:k * 256 + (t + 1) * 128],
                                                   out_w_sb[:, k * 1024 + half * 512:k * 1024 + (half + 1) * 512],
                                                   start=(k == 0), stop=False)
                            ins = e.matmul(out2[:, 0:512], onesrow16[:], out_b_row16[0:1, 0:512], start=False, stop=True)
                            ins = e.matmul(out2[:, 512:1024], onesrow16[:], out_b_row16[0:1, 512:1024], start=False, stop=True)
                            return ins
                        return f
                    p.op("tensor", mk_op(), reads=[f"mixedT_{bb}", "out_w", "onesrow16", "out_b_row16"], writes=["out2", "out2a", "out2b"])
                    yb, ybn = y_sb[t], f"y_sb_{t}"
                    lb, lbn = lny_sb[t], f"lny_sb_{t}"
                    p.op("vector", lambda e, yb=yb, t=t: e.tensor_add(yb[:], out2[:], x3[t][:]),
                         reads=["out2", "out2a", "out2b", f"x3_{t}"], writes=[ybn])
                    sB, N = ln_stats(ybn, yb, (lb, lbn), par=t)
                    p.op("scalar", lambda e, yb=yb, lb=lb, sB=sB: e.activation(lb[:], yb[:], AF.Identity, bias=sB[2][:], scale=sB[5][:]),
                         reads=[ybn, N[2], N[5]], writes=[lbn])
                    p.op("vector", lambda e, lb=lb, sB=sB: e.tensor_reduce(sB[0][:], lb[:], AX.X, ALU.max, apply_absolute_value=True),
                         reads=[lbn], writes=[N[0]])
                    if not lnmax_init[0]:
                        p.op("vector", lambda e, sB=sB: e.tensor_copy(lnmax[:], sB[0][:]), reads=[N[0]], writes=["lnmax"])
                        lnmax_init[0] = True
                    else:
                        p.op("vector", lambda e, sB=sB: e.tensor_max(lnmax[:], lnmax[:], sB[0][:]), reads=[N[0]], writes=["lnmax"])
                    row = b * 256 + t * 128
                    dma("sync", y_spill[q, row:row + 128, :], yb[:], "ySp%d" % t, reads=[ybn])
                    dma("sync", lny_spill[q, row:row + 128, :], lb[:], "lSp%d" % t, reads=[lbn])
        nc._mark_attn = len(p.ops)
        ATT_BAR = bar()

        p.op("tensor", lambda e: e.transpose(mm[0][0:1, 0:128], lnmax[:], identf[:]),
             reads=["lnmax", "identf"], writes=[MMR[0]], waits=[ATT_BAR])
        p.op("vector", lambda e: e.reduce_max(scal[0:1, 0:1], mm[0][0:1, 0:128], AX.X),
             reads=[MMR[0]], writes=["scal"])
        p.op("gpsimd", lambda e: e.dma_start(out=cc1_in, in_=scal[0:1, 0:1]),
             reads=["scal"], sem=dsems["dCC"], amt=16)
        if ncores == 1:
            p.op("gpsimd", lambda e: e.dma_start(out=cc1_out, in_=cc1_in),
                 waits=[aft(dsems["dCC"])], sem=dsems["dCC"], amt=16)
            p.op("gpsimd", lambda e: e.memset(barcell[:], 0.0), waits=[aft(dsems["dCC"])], writes=["barcell"])
        else:
            p.op("gpsimd", lambda e: e.collective_compute(
                "AllReduce", ALU.max, replica_groups=[list(range(ncores))],
                ins=[cc1_in], outs=[cc1_out]), waits=[aft(dsems["dCC"])])
        p.op("gpsimd", lambda e: e.dma_start(out=scal[0:1, 0:1], in_=cc1_out),
             waits=[aft(p.esem["gpsimd"])], writes=["scal"], sem=dsems["dCC"], amt=16)
        p.op("vector", lambda e: e.tensor_scalar(scal[0:1, 1:2], scal[0:1, 0:1], 1e-6, 1.0 / 127.0, ALU.max, ALU.mult),
             reads=["scal"], writes=["scal"])
        p.op("vector", lambda e: e.reciprocal(scal[0:1, 2:3], scal[0:1, 1:2]), reads=["scal"], writes=["scal"])
        p.op("vector", lambda e: e.tensor_mul(scal[0:1, 3:4], scal[0:1, 1:2], wscal[0:1, 1:2]),
             reads=["scal", "wscal"], writes=["scal"])
        bcast_scalar(scal[0:1, 2:3], "scal", rs_b, "rs_b")
        bcast_scalar(scal[0:1, 3:4], "scal", ss1_b, "ss1_b")
        p.op("vector", lambda e: e.tensor_scalar(b1s[:], b1_pm[:], ss1_b[:, 0:1], None, ALU.mult),
             reads=["b1_pm_s", "ss1_b"], writes=["b1s"])
        for kt in range(8):
            for ci in range(2):
                idx = kt * 2 + ci
                buf, bn = wstage[idx % 2], f"wstage_{idx % 2}"
                dma("sync", buf[:], di["w1t"][kt * 128:(kt + 1) * 128, ci * 2048:(ci + 1) * 2048],
                    "dWq%d" % (idx % 2), writes=[bn], waits=[ATT_BAR])
                p.op("scalar", lambda e, buf=buf: e.activation(buf[:], buf[:], AF.Identity,
                                                               scale=rs1_b[:, 0:1], bias=cc_sb[:, CC_MAG:CC_MAG + 1]),
                     reads=[bn, "rs1_b", "cc_sb"], writes=[bn])
                p.op("vector", lambda e, buf=buf, kt=kt, ci=ci: e.tensor_scalar(
                    wq_sb[:, kt * 4096 + ci * 2048:kt * 4096 + (ci + 1) * 2048], buf[:], -MAGIC, None, ALU.add),
                    reads=[bn], writes=["wq_sb"] if idx == 0 else ())
        p.r("wq_sb").w = (p.esem["vector"], p.esem["vector"].n)

        hmax_init = [False]
        NG = NTOK // 512
        for g in range(NG):
            q, gg = divmod(g, 4)
            xq, xqn = xqT[g % 2], f"xqT_{g % 2}"
            for t in range(4):
                row = gg * 512 + t * 128
                qt_, qtn = qtile[t % 2], f"qtile_{t % 2}"
                dma("sync", qt_[:], lny_spill[q, row:row + 128, :], "dQ%d" % (t % 2), writes=[qtn], waits=[ATT_BAR])
                p.op("scalar", lambda e, qt_=qt_: e.activation(qt_[:], qt_[:], AF.Identity,
                                                               scale=rs_b[:, 0:1], bias=cc_sb[:, CC_MAG:CC_MAG + 1]),
                     reads=[qtn, "rs_b", "cc_sb"], writes=[qtn])
                for half in range(2):
                    def mk_trq(qt_=qt_, half=half):
                        def f(e):
                            ins = None
                            for k in range(4):
                                ins = e.transpose(pk[half][:, k * 128:(k + 1) * 128],
                                                  qt_[:, half * 512 + k * 128:half * 512 + (k + 1) * 128], identf[:])
                            return ins
                        return f
                    p.op("tensor", mk_trq(), reads=[qtn, "identf"], writes=[PKR[half]])
                    p.op("vector", lambda e, xq=xq, t=t, half=half: e.tensor_scalar(
                        xq[:].rearrange("p (k s) -> p k s", k=8)[:, half * 4:(half + 1) * 4, t * 128:(t + 1) * 128],
                        pk[half][:].rearrange("p (k s) -> p k s", k=4), -MAGIC, None, ALU.add),
                        reads=[PKR[half]], writes=[xqn] if (t == 0 and half == 0) else ())
                    if not (t == 0 and half == 0):
                        p.r(xqn).w = (p.esem["vector"], p.esem["vector"].n)
            for m_ in range(32):
                bidx = m_ % 3

                def mk_f1(m_=m_, xq=xq, bidx=bidx):
                    def f(e):
                        ins = None
                        for k in range(8):
                            ins = e.matmul(mm[bidx][:, 0:512], wq_sb[:, k * 4096 + m_ * 128:k * 4096 + (m_ + 1) * 128],
                                           xq[:, k * 512:(k + 1) * 512], start=(k == 0), stop=(k == 7))
                        return ins
                    return f
                p.op("tensor", mk_f1(), reads=[xqn, "wq_sb"], writes=[MMR[bidx]])
                hb, hbn = hT_st[m_ % 2], f"hT_st_{m_ % 2}"
                p.op("scalar", lambda e, m_=m_, hb=hb, bidx=bidx: e.activation(
                    hb[:], mm[bidx][:, 0:512], AF.Gelu, scale=ss1_b[:, 0:1], bias=b1s[:, m_:m_ + 1]),
                    reads=[MMR[bidx], "ss1_b", "b1s"], writes=[hbn])
                p.op("vector", lambda e, hb=hb: e.tensor_reduce(st[0][:], hb[:], AX.X, ALU.max, apply_absolute_value=True),
                     reads=[hbn], writes=["st0_"])
                if not hmax_init[0]:
                    p.op("vector", lambda e: e.tensor_copy(hmax[:], st[0][:]), reads=["st0_"], writes=["hmax"])
                    hmax_init[0] = True
                else:
                    p.op("vector", lambda e: e.tensor_max(hmax[:], hmax[:], st[0][:]), reads=["st0_"], writes=["hmax"])
                dma("scalar", h_spill[m_, :, g * 512:(g + 1) * 512], hb[:], "hS%d" % (m_ % 2), reads=[hbn])
        nc._mark_ffna = len(p.ops)
        FFNA_BAR = bar()

        p.op("tensor", lambda e: e.transpose(mm[0][0:1, 0:128], hmax[:], identf[:]),
             reads=["hmax", "identf"], writes=[MMR[0]], waits=[FFNA_BAR])
        p.op("vector", lambda e: e.reduce_max(scal[0:1, 4:5], mm[0][0:1, 0:128], AX.X),
             reads=[MMR[0]], writes=["scal"])
        p.op("gpsimd", lambda e: e.dma_start(out=cc2_in, in_=scal[0:1, 4:5]),
             reads=["scal"], sem=dsems["dCC"], amt=16)
        if ncores == 1:
            p.op("gpsimd", lambda e: e.dma_start(out=cc2_out, in_=cc2_in),
                 waits=[aft(dsems["dCC"])], sem=dsems["dCC"], amt=16)
            p.op("gpsimd", lambda e: e.memset(barcell[:], 0.0), waits=[aft(dsems["dCC"])], writes=["barcell"])
        else:
            p.op("gpsimd", lambda e: e.collective_compute(
                "AllReduce", ALU.max, replica_groups=[list(range(ncores))],
                ins=[cc2_in], outs=[cc2_out]), waits=[aft(dsems["dCC"])])
        p.op("gpsimd", lambda e: e.dma_start(out=scal[0:1, 4:5], in_=cc2_out),
             waits=[aft(p.esem["gpsimd"])], writes=["scal"], sem=dsems["dCC"], amt=16)
        p.op("vector", lambda e: e.tensor_scalar(scal[0:1, 5:6], scal[0:1, 4:5], 1e-6, 1.0 / 127.0, ALU.max, ALU.mult),
             reads=["scal"], writes=["scal"])
        p.op("vector", lambda e: e.reciprocal(scal[0:1, 6:7], scal[0:1, 5:6]), reads=["scal"], writes=["scal"])
        p.op("vector", lambda e: e.tensor_mul(scal[0:1, 7:8], scal[0:1, 5:6], wscal[0:1, 4:5]),
             reads=["scal", "wscal"], writes=["scal"])
        bcast_scalar(scal[0:1, 6:7], "scal", rsh_b, "rsh_b")
        bcast_scalar(scal[0:1, 7:8], "scal", ss2_b, "ss2_b")
        for kt in range(32):
            buf, bn = wstage[kt % 2], f"wstage_{kt % 2}"
            dma("sync", buf[:, 0:1024], di["w2t"][kt * 128:(kt + 1) * 128, :], "dWq%d" % (kt % 2),
                writes=[bn], waits=[FFNA_BAR])
            p.op("scalar", lambda e, buf=buf: e.activation(buf[:, 0:1024], buf[:, 0:1024], AF.Identity,
                                                           scale=rs2_b[:, 0:1], bias=cc_sb[:, CC_MAG:CC_MAG + 1]),
                 reads=[bn, "rs2_b", "cc_sb"], writes=[bn])
            p.op("vector", lambda e, buf=buf, kt=kt: e.tensor_scalar(
                wq_sb[:, kt * 1024:(kt + 1) * 1024], buf[:, 0:1024], -MAGIC, None, ALU.add),
                reads=[bn], writes=["wq_sb"] if kt == 0 else ())
        p.r("wq_sb").w = (p.esem["vector"], p.esem["vector"].n)

        for g2 in range(NTOK // 256):
            q, gg = divmod(g2, 8)
            hq, hqn = hq_sb2[g2 % 2], f"hq_sb_{g2 % 2}"
            for k in range(32):
                sl = k % 4
                buf = hload[sl // 2]
                boff = (sl % 2) * 256
                bn = f"hload_{sl}"
                p.op("scalar", lambda e, buf=buf, boff=boff, k=k, g2=g2: e.dma_start(
                    out=buf[:, boff:boff + 256], in_=h_spill[k, :, g2 * 256:(g2 + 1) * 256]),
                    writes=[bn], waits=[FFNA_BAR], sem=dsems["dH%d" % sl], amt=16)
                p.op("vector", lambda e, buf=buf, boff=boff: e.tensor_scalar(
                    hldt[:, 0:256], buf[:, boff:boff + 256], rsh_b[:, 0:1], MAGIC, ALU.mult, ALU.add),
                    reads=[bn, "rsh_b"], writes=["hldt"])
                p.op("vector", lambda e, k=k, hq=hq: e.tensor_scalar(
                    hq[:, k * 256:(k + 1) * 256], hldt[:, 0:256], -MAGIC, None, ALU.add),
                    reads=["hldt"], writes=[hqn] if k == 0 else ())
            p.r(hqn).w = (p.esem["vector"], p.esem["vector"].n)
            for t in range(2):
                row = gg * 256 + t * 128
                yb, ybn = y_sb[t % 2], f"y_sb_{t % 2}"
                dma("sync", yb[:], y_spill[q, row:row + 128, :], "dY%d" % (t % 2), writes=[ybn], waits=[ATT_BAR])

                def mk_f2(t=t, hq=hq):
                    def f(e):
                        ins = None
                        for k in range(32):
                            for half in range(2):
                                ins = e.matmul(out2[:, half * 512:(half + 1) * 512],
                                               hq[:, k * 256 + t * 128:k * 256 + (t + 1) * 128],
                                               wq_sb[:, k * 1024 + half * 512:k * 1024 + (half + 1) * 512],
                                               start=(k == 0), stop=False)
                        ins = e.matmul(out2[:, 0:512], onesrow16[:], b2_row16[0:1, 0:512], start=False, stop=True)
                        ins = e.matmul(out2[:, 512:1024], onesrow16[:], b2_row16[0:1, 512:1024], start=False, stop=True)
                        return ins
                    return f
                p.op("tensor", mk_f2(), reads=[hqn, "wq_sb", "onesrow16", "b2_row16"], writes=["out2", "out2a", "out2b"])
                ob, obn = oout[t % 2], f"oout_{t % 2}"
                p.op("vector", lambda e, ob=ob, yb=yb: e.scalar_tensor_tensor(
                    ob[:], out2[:], ss2_b[:, 0:1], yb[:], ALU.mult, ALU.add),
                    reads=["out2", "out2a", "out2b", "ss2_b", ybn], writes=[obn])
                dma("sync", out_d[q, row:row + 128, :], ob[:], "oS%d" % (t % 2), reads=[obn])
        p.op("sync", None, waits=[aft(s_) for s_ in dsems.values()])
        p.op("gpsimd", None, waits=[aft(dsems["oS0"]), aft(dsems["oS1"])])

        if trunc is not None:
            p.ops = p.ops[:trunc]
        with nc.Block() as block:
            p.emit(block)

    return nc


def kernel(**inputs):
    x = np.ascontiguousarray(np.asarray(inputs["L0_attn_x"], dtype=np.float32))
    consts = build_host_consts()
    wz = build_weight_inputs(inputs)
    nseq = 4
    nc = build_program(nseq=nseq)
    in_maps = []
    for c in range(NCORES):
        m = dict(consts)
        m.update(wz)
        m["x"] = np.ascontiguousarray(x[c * nseq:(c + 1) * nseq])
        in_maps.append(m)
    res = run_bass_kernel_spmd(nc, in_maps, core_ids=list(range(NCORES)))
    out = np.concatenate([np.asarray(r["out"]) for r in res.results], axis=0)
    return out.reshape(32, 2048, 1024)


# version 13 (final)
# speedup vs baseline: 1.0155x; 1.0155x over previous
import numpy as np
import contextlib
import concourse.bass as bass
import concourse.mybir as mybir
from concourse.bass_utils import run_bass_kernel_spmd

F32 = mybir.dt.float32
BF16 = mybir.dt.bfloat16
AF = mybir.ActivationFunctionType
ALU = mybir.AluOpType
AX = mybir.AxisListType

MAGIC = float(np.float32(1.5 * 2 ** 23))
NCORES = 8
S, D, NH, HD, NB, BS = 2048, 1024, 16, 64, 8, 256
NLAT, CH, LH, H1 = 64, 32, 4, 4096

CC_SCL, CC_1_1024, CC_EPS, CC_MAG, CC_NMAG, CC_1_32, CC_ONE = range(7)


class Sem:
    __slots__ = ("h", "n", "name")

    def __init__(self, h, name):
        self.h, self.n, self.name = h, 0, name


class Res:
    __slots__ = ("name", "w", "readers")

    def __init__(self, name):
        self.name, self.w, self.readers = name, None, []


class Prog:
    ENGINES = ("sync", "gpsimd", "vector", "scalar", "tensor")

    def __init__(self, nc, stack):
        self.nc, self.stack = nc, stack
        self.ops = []
        self.esem = {}
        self.res = {}

    def sem(self, name):
        return Sem(self.stack.enter_context(self.nc.semaphore(name)), name)

    def r(self, name):
        if name not in self.res:
            self.res[name] = Res(name)
        return self.res[name]

    ALIASES = {}

    def op(self, eng, fn, reads=(), writes=(), waits=(), sem=None, amt=1):
        writes = list(writes)
        for w in list(writes):
            writes.extend(self.ALIASES.get(w, ()))
        if fn is None:
            assert not reads and not writes
            self.ops.append((eng, None, tuple(waits), None, 0))
            return
        if sem is None:
            sem = self.esem[eng]
        if sem is None:
            sem_ = self.esem[eng]
        else:
            sem_ = sem
        wl = list(waits)
        for rn in reads:
            rr = self.r(rn)
            if rr.w is not None:
                wl.append(rr.w)
        for wn in writes:
            rr = self.r(wn)
            if rr.w is not None and rr.w[0] is not sem_:
                wl.append(rr.w)
            wl.extend(rr.readers)
        sem = sem_
        self.ops.append((eng, fn, tuple(wl), sem, amt))
        sem.n += amt
        ident = (sem, sem.n)
        for rn in reads:
            self.r(rn).readers.append(ident)
        for wn in writes:
            rr = self.r(wn)
            rr.w = ident
            rr.readers = []

    def emit(self, block):
        for eng_name in self.ENGINES:
            ops = [o for o in self.ops if o[0] == eng_name]

            def body(eng, ops=ops):
                waited = {}
                for _, fn, waits, sem, amt in ops:
                    best = {}
                    for s, val in waits:
                        if val > 0 and val > best.get(id(s), (None, -1))[1]:
                            best[id(s)] = (s, val)
                    for s, val in best.values():
                        if waited.get(id(s), -1) < val:
                            eng.wait_ge(s.h, val)
                            waited[id(s)] = val
                    ins = fn(eng) if fn is not None else None
                    if ins is not None and sem is not None:
                        ins.then_inc(sem.h, amt)
            getattr(block, eng_name)(body)


def build_host_consts():
    c = {}
    c["ident"] = np.eye(128, dtype=np.float32)
    i = np.arange(128)
    j2 = np.arange(256)
    m = np.zeros((128, 512), dtype=np.float32)
    m[:, 0:128] = (np.arange(128)[None, :] <= i[:, None])
    m[:, 128:384] = (i[:, None] < j2[None, :]) & (i[:, None] >= j2[None, :] - 128)
    m[:, 384:512] = (np.arange(128, 256)[None, :] >= i[:, None] + 129)
    c["masks"] = m
    m0 = m.copy()
    m0[:, 0:128] = 0.0
    c["masks0"] = m0
    lat = np.zeros((64, NB * 256), dtype=np.float32)
    l = np.arange(NLAT)
    for b in range(NB):
        s_idx = b * BS + np.arange(BS)
        lat[:, b * 256:(b + 1) * 256] = (l[:, None] < (s_idx[None, :] // CH)).astype(np.float32)
    c["latmask"] = lat
    oc = np.zeros((128, 256), dtype=np.float32)
    for h in range(16):
        oc[:, h * 16 + h] = 1.0
    c["onescol"] = oc
    ps = np.zeros((16, 1024), dtype=np.float32)
    for p_ in range(8):
        ps[2 * p_, p_ * 128:p_ * 128 + 64] = 1.0
        ps[2 * p_ + 1, p_ * 128 + 64:p_ * 128 + 128] = 1.0
    c["pairsel"] = ps
    lps = np.zeros((4, 256), dtype=np.float32)
    for p_ in range(2):
        lps[2 * p_, p_ * 128:p_ * 128 + 64] = 1.0
        lps[2 * p_ + 1, p_ * 128 + 64:p_ * 128 + 128] = 1.0
    c["latpairsel"] = lps
    c["onesrow"] = np.ones((1, 128), dtype=np.float32)
    cc = np.zeros((128, 8), dtype=np.float32)
    cc[:, CC_SCL] = 0.125
    cc[:, CC_1_1024] = 1.0 / 1024.0
    cc[:, CC_EPS] = 1e-5
    cc[:, CC_MAG] = MAGIC
    cc[:, CC_NMAG] = -MAGIC
    cc[:, CC_1_32] = 1.0 / 32.0
    cc[:, CC_ONE] = 1.0
    c["cc"] = cc
    return c


def build_weight_inputs(w):
    d = {}
    d["qkv_w"] = np.ascontiguousarray(w["L0_attn_qkv_w"])
    d["qkv_b_pm"] = np.ascontiguousarray(np.asarray(w["L0_attn_qkv_b"]).reshape(24, 128).T)
    d["qkv_b_row"] = np.ascontiguousarray(np.asarray(w["L0_attn_qkv_b"])[None, :])
    d["rq_w"] = np.ascontiguousarray(w["L0_attn_rq_w"])
    d["rq_b_pm"] = np.ascontiguousarray(np.asarray(w["L0_attn_rq_b"]).reshape(2, 128).T)
    d["rkv_w"] = np.ascontiguousarray(w["L0_attn_rkv_w"])
    d["rkv_b_pm"] = np.ascontiguousarray(np.asarray(w["L0_attn_rkv_b"])[0:256].reshape(2, 128).T)
    d["rkv_b_rv_row"] = np.ascontiguousarray(np.asarray(w["L0_attn_rkv_b"])[None, 256:512])
    d["rout_w"] = np.ascontiguousarray(w["L0_attn_rout_w"])
    d["rout_b_pm"] = np.ascontiguousarray(np.asarray(w["L0_attn_rout_b"]).reshape(8, 128).T)
    d["out_w"] = np.ascontiguousarray(w["L0_attn_out_w"])
    d["out_b_row"] = np.ascontiguousarray(np.asarray(w["L0_attn_out_b"])[None, :])
    d["w1t"] = np.ascontiguousarray(np.asarray(w["L0_ffn_w1"]).T)
    d["b1_pm"] = np.ascontiguousarray(np.asarray(w["L0_ffn_b1"]).reshape(32, 128).T)
    d["w2t"] = np.ascontiguousarray(np.asarray(w["L0_ffn_w2"]).T)
    d["b2_row"] = np.ascontiguousarray(np.asarray(w["L0_ffn_b2"])[None, :])
    gl = np.asarray(w["L0_attn_gl_w"])[:, 0]
    gr = np.asarray(w["L0_attn_gr_w"])[:, 0]
    gp = np.zeros((128, 256), dtype=np.float32)
    rp = np.zeros((128, 256), dtype=np.float32)
    for h in range(16):
        par = h % 2
        gp[par * 64:(par + 1) * 64, h * 16 + h] = gl
        rp[par * 64:(par + 1) * 64, h * 16 + h] = gr
    d["glpat"] = gp
    d["grpat"] = rp
    return d


def build_program(nseq=4, debug=False, ncores=NCORES, trunc=None):
    nc = bass.Bass("TRN2", target_bir_lowering=False)
    di = {}

    def inp(name, shape):
        di[name] = nc.dram_tensor(name, list(shape), F32, kind="ExternalInput").ap()

    inp("x", (nseq, S, D))
    inp("qkv_w", (D, 3 * D)); inp("qkv_b_pm", (128, 24)); inp("qkv_b_row", (1, 3 * D))
    inp("rq_w", (D, 256)); inp("rq_b_pm", (128, 2))
    inp("rkv_w", (D, 512)); inp("rkv_b_pm", (128, 2)); inp("rkv_b_rv_row", (1, 256))
    inp("rout_w", (256, D)); inp("rout_b_pm", (128, 8))
    inp("out_w", (D, D)); inp("out_b_row", (1, D))
    inp("w1t", (D, H1)); inp("b1_pm", (128, 32))
    inp("w2t", (H1, D)); inp("b2_row", (1, D))
    inp("ident", (128, 128)); inp("masks", (128, 512)); inp("masks0", (128, 512))
    inp("latmask", (64, NB * 256)); inp("onescol", (128, 256))
    inp("pairsel", (16, 1024)); inp("latpairsel", (4, 256)); inp("onesrow", (1, 128))
    inp("cc", (128, 8)); inp("glpat", (128, 256)); inp("grpat", (128, 256))

    out_d = nc.dram_tensor("out", [nseq, S, D], F32, kind="ExternalOutput").ap()
    kd = "ExternalOutput" if debug else "Internal"
    xt_spill = nc.dram_tensor("xt_spill", [nseq, 8, 128, S], BF16).ap()
    y_spill = nc.dram_tensor("y_spill", [nseq, S, D], F32, kind=kd).ap()
    lny_spill = nc.dram_tensor("lny_spill", [nseq, S, D], F32, kind=kd).ap()
    h_spill = nc.dram_tensor("h_spill", [32, 128, nseq * S], F32, kind=kd).ap()
    cc1_in = nc.dram_tensor("cc1_in", [1, 1], F32).ap()
    cc1_out = nc.dram_tensor("cc1_out", [1, 1], F32).ap()
    cc2_in = nc.dram_tensor("cc2_in", [1, 1], F32).ap()
    cc2_out = nc.dram_tensor("cc2_out", [1, 1], F32).ap()

    NTOK = nseq * S

    with contextlib.ExitStack() as stack:
        E = stack.enter_context
        p = Prog(nc, stack)

        def sb(name, shape, dt=BF16):
            return E(nc.sbuf_tensor(name, list(shape), dt))

        ident16 = sb("ident16", (128, 128)); identf = sb("identf", (128, 128), F32)
        onescol16 = sb("onescol16", (128, 256))
        pairsel16 = sb("pairsel16", (16, 1024)); latpairsel16 = sb("latpairsel16", (4, 256))
        onesrow16 = sb("onesrow16", (1, 128))
        glpat16 = sb("glpat16", (128, 256)); grpat16 = sb("grpat16", (128, 256))
        cc_sb = sb("cc_sb", (128, 8), F32)
        qkv_b_pm = sb("qkv_b_pm_s", (128, 24), F32)
        qkv_b_vrow = sb("qkv_b_vrow", (1, 1024))
        rq_b_pm = sb("rq_b_pm_s", (128, 2), F32)
        rkv_b_pm = sb("rkv_b_pm_s", (128, 2), F32)
        rkv_b_rv_row = sb("rkv_b_rv_row_s", (1, 256), F32)
        rkv_b_rv32 = sb("rkv_b_rv32", (1, 256))
        rout_b_pm = sb("rout_b_pm_s", (128, 8), F32)
        out_b_row16 = sb("out_b_row16", (1, D))
        b1_pm = sb("b1_pm_s", (128, 32), F32); b1s = sb("b1s", (128, 32), F32)
        b2_row16 = sb("b2_row16", (1, D))
        x3 = [sb(f"x3_{i}", (128, D), F32) for i in range(2)]
        y_sb = [sb(f"y_sb_{i}", (128, D), F32) for i in range(2)]
        lny_sb = [sb(f"lny_sb_{i}", (128, D), F32) for i in range(2)]
        st = [sb(f"st_{i}", (128, 1), F32) for i in range(12)]
        lnmax = sb("lnmax", (128, 1), F32); hmax = sb("hmax", (128, 1), F32)
        w1max = sb("w1max", (128, 1), F32); w2max = sb("w2max", (128, 1), F32)
        rloc = sb("rloc", (16, 256), F32)
        gsig = sb("gsig", (16, 256), F32); gtmp = sb("gtmp", (16, 256), F32)
        apack = sb("apack", (16, 256)); bpack = sb("bpack", (16, 256))
        rlat16 = sb("rlat16", (4, 256)); rlatf = sb("rlatf", (4, 256), F32)
        scal = sb("scal", (1, 8), F32); wscal = sb("wscal", (1, 8), F32)
        barcell = sb("barcell", (1, 1), F32)
        onesf_row = sb("onesf_row", (1, 128), F32)
        rs_b = sb("rs_b", (128, 1), F32); rsh_b = sb("rsh_b", (128, 1), F32)
        ss1_b = sb("ss1_b", (128, 1), F32); ss2_b = sb("ss2_b", (128, 1), F32)
        rs1_b = sb("rs1_b", (128, 1), F32); rs2_b = sb("rs2_b", (128, 1), F32)

        abase = (nc.sbuf_base + 31) & ~31
        ARENA = 229344 - abase

        def sba(name, shape, dt, off):
            assert off + shape[1] * mybir.dt.size(dt) <= ARENA, name
            return nc.alloc_sbuf_tensor_at(name, list(shape), dt, offset=abase + off)

        o = 0
        masks16 = sba("masks16", (128, 512), BF16, o); o += 1024
        masks016 = sba("masks016", (128, 512), BF16, o); o += 1024
        latmask16 = sba("latmask16", (64, NB * 256), BF16, o); o += 4 * 1024
        qkv_w_sb = sba("qkv_w_sb", (128, 8 * 3072), BF16, o); o += 48 * 1024
        out_w_sb = sba("out_w_sb", (128, 8 * 1024), BF16, o); o += 16 * 1024
        rkv_w_sb = sba("rkv_w_sb", (128, 8 * 512), BF16, o); o += 8 * 1024
        rq_w_sb = sba("rq_w_sb", (128, 8 * 256), BF16, o); o += 4 * 1024
        rout_w_sb = sba("rout_w_sb", (128, 2 * 1024), BF16, o); o += 4 * 1024
        rqT = sba("rqT", (128, 2 * S), BF16, o); o += 8 * 1024
        pooledT = sba("pooledT", (128, 8 * 64), BF16, o); o += 1024
        rkT = sba("rkT", (128, 2 * 64), BF16, o); o += 256
        rv_sb = sba("rv_sb", (64, 256), BF16, o); o += 512
        kt_ring = sba("kt_ring", (128, 8 * 512), BF16, o); o += 8 * 1024
        v_ring = sba("v_ring", (128, 4 * 1024), BF16, o); o += 8 * 1024
        TRANS0 = o
        qt_sb = [sba(f"qt_{i}", (128, 8 * 256), BF16, o + i * 4096) for i in range(2)]; o += 8 * 1024
        pexp = [sba(f"pexp_{i}", (128, 512), BF16, o + i * 1024) for i in range(2)]; o += 2 * 1024
        pmask = [sba(f"pmask_{i}", (128, 512), BF16, o + i * 1024) for i in range(2)]; o += 2 * 1024
        localT = [sba(f"localT_{i}", (128, 8 * 256), BF16, o + i * 4096) for i in range(2)]; o += 8 * 1024
        remoteT = [sba(f"remoteT_{i}", (128, 8 * 256), BF16, o + i * 4096) for i in range(2)]; o += 8 * 1024
        mixedT = [sba(f"mixedT_{i}", (128, 8 * 256), BF16, o + i * 4096) for i in range(2)]; o += 8 * 1024
        latn = [sba(f"latn_{i}", (128, 512), BF16, o + i * 1024) for i in range(2)]; o += 2 * 1024
        platm = [sba(f"platm_{i}", (64, 1024), BF16, o + i * 2048) for i in range(2)]; o += 4 * 1024
        mxt = [sba(f"mxt_{i}", (128, 256), BF16, o + i * 512) for i in range(2)]; o += 1024
        xtb = [sba(f"xtb_{i}", (128, 8 * 256), BF16, o + i * 4096) for i in range(2)]; o += 8 * 1024
        o1 = TRANS0
        x1 = [sba(f"x1_{i}", (128, D), F32, o1 + i * 4096) for i in range(2)]; o1 += 8 * 1024
        lnx = [sba(f"lnx_{i}", (128, D), BF16, o1 + i * 2048) for i in range(2)]; o1 += 4 * 1024
        xtg = [sba(f"xtg_{i}", (128, 8 * 512), BF16, o1 + i * 8192) for i in range(2)]; o1 += 16 * 1024
        pooledf = sba("pooledf", (128, 512), F32, o1); o1 += 2 * 1024
        assert o1 <= o <= ARENA
        o2 = 0
        wq_sb = sba("wq_sb", (128, 32768), BF16, o2); o2 += 64 * 1024
        wstage = [sba(f"wstage_{i}", (128, 2048), F32, o2 + i * 8192) for i in range(2)]; o2 += 16 * 1024
        hq_sb2 = [sba(f"hq_sb_{i}", (128, 32 * 256), BF16, o2 + i * 16 * 1024) for i in range(2)]; o2 += 32 * 1024
        qtile = [sba(f"qtile_{i}", (128, D), F32, o2 + i * 4096) for i in range(2)]; o2 += 8 * 1024
        xqT = [sba(f"xqT_{i}", (128, 8 * 512), BF16, o2 + i * 8192) for i in range(2)]; o2 += 16 * 1024
        hT_st = [sba(f"hT_st_{i}", (128, 512), F32, o2 + i * 2048) for i in range(2)]; o2 += 4 * 1024
        hload = [sba(f"hload_{i}", (128, 512), F32, o2 + i * 2048) for i in range(2)]; o2 += 4 * 1024
        hldt = sba("hldt", (128, 256), F32, o2); o2 += 1024
        oout = [sba(f"oout_{i}", (128, D), F32, o2 + i * 4096) for i in range(2)]; o2 += 8 * 1024
        assert o2 <= ARENA

        mm = [E(nc.psum_tensor(f"mm_{i}", [128, 512], F32)) for i in range(3)]
        pk = [E(nc.psum_tensor(f"pk_{i}", [128, 512], F32)) for i in range(2)]
        out2 = E(nc.psum_tensor("out2", [128, 1024], F32))
        trb = E(nc.psum_tensor("trb", [128, 1024], BF16))
        MMR = ["mm0", "mm1", "mm2"]
        PKR = ["pk0", "pk1"]

        p.ALIASES.update({
            "x1_0": ["qt_0"], "qt_0": ["x1_0"],
            "x1_1": ["qt_1"], "qt_1": ["x1_1"],
            "lnx_0": ["pexp_0", "pexp_1"], "pexp_0": ["lnx_0"], "pexp_1": ["lnx_0"],
            "lnx_1": ["pmask_0", "pmask_1"], "pmask_0": ["lnx_1"], "pmask_1": ["lnx_1"],
            "xtg_0": ["localT_0", "localT_1"], "localT_0": ["xtg_0"], "localT_1": ["xtg_0"],
            "xtg_1": ["remoteT_0", "remoteT_1"], "remoteT_0": ["xtg_1"], "remoteT_1": ["xtg_1"],
            "pooledf": ["mixedT_0"], "mixedT_0": ["pooledf"],
        })
        for en in Prog.ENGINES:
            p.esem[en] = p.sem("e_" + en)
        sX = p.esem["gpsimd"]
        dsems = {}
        for n_ in ("dW", "dCst", "dX0", "dX1", "dA0", "dA1", "dB0", "dB1",
                   "dY0", "dY1", "dQ0", "dQ1", "dH0", "dH1", "dH2", "dH3", "dWq0", "dWq1",
                   "xS0", "xS1", "ySp0", "ySp1", "lSp0", "lSp1",
                   "hS0", "hS1", "oS0", "oS1", "dXT", "dSP", "dHS", "dO", "dCC"):
            dsems[n_] = p.sem(n_)

        def dma(queue, out, in_, semname, reads=(), writes=(), waits=()):
            p.op(queue, lambda e, o=out, i=in_: e.dma_start(out=o, in_=i),
                 reads=reads, writes=writes, waits=waits, sem=dsems[semname], amt=16)

        def aft(sem):
            return (sem, sem.n)

        def bcast_scalar(src_ap, src_res, dst, dst_res):
            p.op("tensor", lambda e, src_ap=src_ap: e.matmul(mm[0][:, 0:1], onesf_row[:], src_ap, start=True, stop=True),
                 reads=[src_res, "onesf_row"], writes=[MMR[0]])
            p.op("vector", lambda e, dst=dst: e.tensor_copy(dst[:], mm[0][:, 0:1]),
                 reads=[MMR[0]], writes=[dst_res])

        def bar():
            w = [aft(s_) for s_ in dsems.values()] + [aft(p.esem[e]) for e in Prog.ENGINES if e != "vector"]
            p.op("vector", lambda e: e.memset(barcell[:], 0.0), waits=w, writes=["barcell"])
            return aft(p.esem["vector"])

        for name, dst in [("ident", ident16), ("masks", masks16), ("masks0", masks016),
                          ("latmask", latmask16), ("onescol", onescol16), ("pairsel", pairsel16),
                          ("latpairsel", latpairsel16), ("onesrow", onesrow16),
                          ("glpat", glpat16), ("grpat", grpat16), ("b2_row", b2_row16),
                          ("out_b_row", out_b_row16), ("rkv_b_rv_row", rkv_b_rv_row)]:
            dma("gpsimd", dst[:], di[name], "dW", writes=[dst.name])
        dma("gpsimd", qkv_b_vrow[:], di["qkv_b_row"][0:1, 2048:3072], "dW", writes=["qkv_b_vrow"])
        for name, dst in [("ident", identf), ("cc", cc_sb), ("qkv_b_pm", qkv_b_pm),
                          ("rq_b_pm", rq_b_pm), ("rkv_b_pm", rkv_b_pm), ("rout_b_pm", rout_b_pm),
                          ("b1_pm", b1_pm)]:
            dma("sync", dst[:], di[name], "dCst", writes=[dst.name if dst.name else name])
        for k in range(8):
            dma("gpsimd", qkv_w_sb[:, k * 3072:(k + 1) * 3072], di["qkv_w"][k * 128:(k + 1) * 128, :], "dW", writes=["qkv_w"])
            dma("gpsimd", out_w_sb[:, k * 1024:(k + 1) * 1024], di["out_w"][k * 128:(k + 1) * 128, :], "dW", writes=["out_w"])
            dma("gpsimd", rkv_w_sb[:, k * 512:(k + 1) * 512], di["rkv_w"][k * 128:(k + 1) * 128, :], "dW", writes=["rkv_w"])
            dma("gpsimd", rq_w_sb[:, k * 256:(k + 1) * 256], di["rq_w"][k * 128:(k + 1) * 128, :], "dW", writes=["rq_w"])
        for k in range(2):
            dma("gpsimd", rout_w_sb[:, k * 1024:(k + 1) * 1024], di["rout_w"][k * 128:(k + 1) * 128, :], "dW", writes=["rout_w"])
        for rr_ in p.res.values():
            if rr_.w is not None and rr_.w[0] in (dsems["dW"], dsems["dCst"]):
                rr_.w = (rr_.w[0], rr_.w[0].n)
        p.op("vector", lambda e: e.tensor_scalar(rkv_b_rv32[:], rkv_b_rv_row[:], 32.0, None, ALU.mult),
             reads=["rkv_b_rv_row_s"], writes=["rkv_b_rv32"])
        p.op("vector", lambda e: e.memset(onesf_row[:], 1.0), writes=["onesf_row"])
        p.op("vector", lambda e: e.memset(kt_ring[:], 0.0), writes=["ktr0", "ktr1", "ktr2", "ktr3"])
        p.op("vector", lambda e: e.memset(v_ring[:], 0.0), writes=["vr0", "vr1", "vr2", "vr3"])

        def wabs_stream(wname, nrt, ncol, maxres, maxbuf):
            first = True
            idx = 0
            for kt in range(nrt):
                for ci in range(ncol // 1024):
                    buf = y_sb[idx % 2]
                    bn = f"y_sb_{idx % 2}"
                    dma("sync", buf[:], di[wname][kt * 128:(kt + 1) * 128, ci * 1024:(ci + 1) * 1024],
                        "dY%d" % (idx % 2), writes=[bn])
                    if first:
                        p.op("vector", lambda e, b=buf: e.tensor_reduce(maxbuf[:], b[:], AX.X, ALU.max, apply_absolute_value=True),
                             reads=[bn], writes=[maxres])
                        first = False
                    else:
                        p.op("vector", lambda e, b=buf: e.tensor_reduce(st[0][:], b[:], AX.X, ALU.max, apply_absolute_value=True),
                             reads=[bn], writes=["st0"])
                        p.op("vector", lambda e: e.tensor_max(maxbuf[:], maxbuf[:], st[0][:]),
                             reads=["st0"], writes=[maxres])
                    idx += 1
        wabs_stream("w1t", 8, H1, "w1max", w1max)
        wabs_stream("w2t", 32, D, "w2max", w2max)
        p.op("tensor", lambda e: e.transpose(mm[0][0:1, 0:128], w1max[:], identf[:]),
             reads=["w1max", "identf"], writes=["mm0"])
        p.op("vector", lambda e: e.reduce_max(wscal[0:1, 0:1], mm[0][0:1, 0:128], AX.X),
             reads=["mm0"], writes=["wscal"])
        p.op("tensor", lambda e: e.transpose(mm[0][0:1, 128:256], w2max[:], identf[:]),
             reads=["w2max", "identf"], writes=["mm0"])
        p.op("vector", lambda e: e.reduce_max(wscal[0:1, 3:4], mm[0][0:1, 128:256], AX.X),
             reads=["mm0"], writes=["wscal"])
        p.op("vector", lambda e: e.tensor_scalar(wscal[0:1, 1:2], wscal[0:1, 0:1], 1e-6, 1.0 / 127.0, ALU.max, ALU.mult), reads=["wscal"], writes=["wscal"])
        p.op("vector", lambda e: e.reciprocal(wscal[0:1, 2:3], wscal[0:1, 1:2]), reads=["wscal"], writes=["wscal"])
        p.op("vector", lambda e: e.tensor_scalar(wscal[0:1, 4:5], wscal[0:1, 3:4], 1e-6, 1.0 / 127.0, ALU.max, ALU.mult), reads=["wscal"], writes=["wscal"])
        p.op("vector", lambda e: e.reciprocal(wscal[0:1, 5:6], wscal[0:1, 4:5]), reads=["wscal"], writes=["wscal"])
        bcast_scalar(wscal[0:1, 2:3], "wscal", rs1_b, "rs1_b")
        bcast_scalar(wscal[0:1, 5:6], "wscal", rs2_b, "rs2_b")

        def ln_stats(src_name, src, junk, par=0):
            o6 = par * 6
            sB = st[o6:o6 + 6]
            N = [f"st{o6 + i}" for i in range(6)]
            p.op("vector", lambda e, sB=sB: e.reduce_sum(sB[0][:], src[:], AX.X), reads=[src_name], writes=[N[0]])
            p.op("scalar", lambda e, sB=sB: e.activation(junk[0][:], src[:], AF.Square, accum_out=sB[1][:]),
                 reads=[src_name], writes=[junk[1], N[1]])
            p.op("vector", lambda e, sB=sB: e.tensor_scalar(sB[2][:], sB[0][:], 1.0 / 1024.0, None, ALU.mult), reads=[N[0]], writes=[N[2]])
            p.op("vector", lambda e, sB=sB: e.tensor_scalar(sB[3][:], sB[1][:], 1.0 / 1024.0, None, ALU.mult), reads=[N[1]], writes=[N[3]])
            p.op("vector", lambda e, sB=sB: e.tensor_mul(sB[4][:], sB[2][:], sB[2][:]), reads=[N[2]], writes=[N[4]])
            p.op("vector", lambda e, sB=sB: e.tensor_sub(sB[3][:], sB[3][:], sB[4][:]), reads=[N[3], N[4]], writes=[N[3]])
            p.op("scalar", lambda e, sB=sB: e.activation(sB[4][:], sB[3][:], AF.Sqrt, bias=cc_sb[:, CC_EPS:CC_EPS + 1]),
                 reads=[N[3], "cc_sb"], writes=[N[4]])
            p.op("vector", lambda e, sB=sB: e.reciprocal(sB[5][:], sB[4][:]), reads=[N[4]], writes=[N[5]])
            p.op("vector", lambda e, sB=sB: e.tensor_scalar(sB[2][:], sB[2][:], sB[5][:], -1.0, ALU.mult, ALU.mult),
                 reads=[N[2], N[5]], writes=[N[2]])
            return sB, N

        nc._mark_p0 = len(p.ops)
        lnmax_init = [False]
        for q in range(nseq):
            for g in range(4):
                gx = xtg[g % 2]
                gxn = f"xtg_{g % 2}"
                for t in range(4):
                    T = g * 4 + t
                    xb, xbn = x1[T % 2], f"x1_{T % 2}"
                    dma("sync", xb[:], di["x"][q, T * 128:(T + 1) * 128, :], "dX%d" % (T % 2), writes=[xbn])
                    sB, N = ln_stats(xbn, xb, (lny_sb[T % 2], f"lny_sb_{T % 2}"), par=T % 2)
                    lb, lbn = lnx[T % 2], f"lnx_{T % 2}"
                    p.op("scalar", lambda e, xb=xb, lb=lb, sB=sB: e.activation(lb[:], xb[:], AF.Identity, bias=sB[2][:], scale=sB[5][:]),
                         reads=[xbn, N[2], N[5]], writes=[lbn])

                    def mk_tr(lb=lb):
                        def f(e):
                            ins = None
                            for k in range(8):
                                ins = e.transpose(trb[:, k * 128:(k + 1) * 128], lb[:, k * 128:(k + 1) * 128], ident16[:])
                            return ins
                        return f
                    p.op("tensor", mk_tr(), reads=[lbn, "ident16"], writes=["trb"])
                    p.op("vector", lambda e, gx=gx, t=t: e.tensor_copy(
                        gx[:].rearrange("p (k s) -> p k s", k=8)[:, :, t * 128:(t + 1) * 128],
                        trb[:].rearrange("p (k s) -> p k s", k=8)),
                        reads=["trb"], writes=[gxn])
                for pr in range(2):
                    bank = MMR[2]

                    def mk_rq(pr=pr, gx=gx):
                        def f(e):
                            ins = None
                            for k in range(8):
                                ins = e.matmul(mm[2][:, 0:512], rq_w_sb[:, k * 256 + pr * 128:k * 256 + (pr + 1) * 128],
                                               gx[:, k * 512:(k + 1) * 512], start=(k == 0), stop=(k == 7))
                            return ins
                        return f
                    p.op("tensor", mk_rq(), reads=[gxn, "rq_w"], writes=[bank])
                    p.op("scalar", lambda e, pr=pr, g=g: e.activation(
                        rqT[:, pr * S + g * 512:pr * S + (g + 1) * 512], mm[2][:, 0:512], AF.Identity,
                        bias=rq_b_pm[:, pr:pr + 1]), reads=[bank, "rq_b_pm_s"], writes=["rqT"])

                def mk_pool(gx=gx, g=g):
                    def f(e):
                        ins = None
                        for k in range(8):
                            ins = e.reduce_sum(pooledf[:, k * 64 + g * 16:k * 64 + (g + 1) * 16],
                                               gx[:, k * 512:(k + 1) * 512].rearrange("p (l w) -> p l w", w=32), AX.X)
                        return ins
                    return f
                p.op("vector", mk_pool(), reads=[gxn], writes=["pooledf"])
                for k in range(8):
                    dma("sync", xt_spill[q, k, :, g * 512:(g + 1) * 512], gx[:, k * 512:(k + 1) * 512],
                        "xS%d" % (g % 2), reads=[gxn])
            XT_READY0 = aft(dsems["xS0"])
            XT_READY1 = aft(dsems["xS1"])

            p.op("vector", lambda e: e.tensor_copy(pooledT[:], pooledf[:]), reads=["pooledf"], writes=["pooledT"])
            for pr in range(2):
                def mk_rk(pr=pr):
                    def f(e):
                        ins = None
                        for k in range(8):
                            ins = e.matmul(mm[0][:, 0:64], rkv_w_sb[:, k * 512 + pr * 128:k * 512 + (pr + 1) * 128],
                                           pooledT[:, k * 64:(k + 1) * 64], start=(k == 0), stop=(k == 7))
                        return ins
                    return f
                p.op("tensor", mk_rk(), reads=["pooledT", "rkv_w"], writes=[MMR[0]])
                p.op("scalar", lambda e, pr=pr: e.activation(
                    rkT[:, pr * 64:(pr + 1) * 64], mm[0][:, 0:64], AF.Identity,
                    bias=rkv_b_pm[:, pr:pr + 1], scale=cc_sb[:, CC_1_32:CC_1_32 + 1]),
                    reads=[MMR[0], "rkv_b_pm_s", "cc_sb"], writes=["rkT"])

            def mk_rv(e):
                ins = None
                for k in range(8):
                    ins = e.matmul(mm[1][0:64, 0:256], pooledT[:, k * 64:(k + 1) * 64],
                                   rkv_w_sb[:, k * 512 + 256:k * 512 + 512], start=(k == 0), stop=False)
                ins = e.matmul(mm[1][0:64, 0:256], onesrow16[0:1, 0:64], rkv_b_rv32[:], start=False, stop=True)
                return ins
            p.op("tensor", mk_rv, reads=["pooledT", "rkv_w", "onesrow16", "rkv_b_rv32"], writes=[MMR[1]])
            p.op("scalar", lambda e: e.activation(rv_sb[:, :], mm[1][0:64, 0:256], AF.Identity,
                                                  scale=cc_sb[0:64, CC_1_32:CC_1_32 + 1]),
                 reads=[MMR[1], "cc_sb"], writes=["rv_sb"])

            for b in range(NB):
                bb = b % 2
                pkb = PKR[bb]
                msk, mskn = (masks016, "masks016") if b == 0 else (masks16, "masks16")
                for k in range(8):
                    dma("sync", xtb[bb][:, k * 256:(k + 1) * 256], xt_spill[q, k, :, b * 256:(b + 1) * 256],
                        "dB%d" % bb, writes=[f"xtb_{bb}"] if k == 0 else (), reads=() if k else (),
                        waits=[XT_READY0, XT_READY1])
                p.r(f"xtb_{bb}").w = (dsems["dB%d" % bb], dsems["dB%d" % bb].n)
                for t in range(2):
                    dma("sync", x3[t][:], di["x"][q, b * 256 + t * 128:b * 256 + (t + 1) * 128, :],
                        "dA%d" % t, writes=[f"x3_{t}"])

                pack_ops = []

                for lh in range(4):
                    pr, par = lh // 2, lh % 2
                    bank = MMR[lh % 2]

                    def mk_lat(pr=pr, par=par, b=b, lh=lh):
                        def f(e):
                            return e.matmul(mm[lh % 2][0:64, 0:256],
                                            rkT[par * 64:(par + 1) * 64, pr * 64:(pr + 1) * 64],
                                            rqT[par * 64:(par + 1) * 64, pr * S + b * 256:pr * S + (b + 1) * 256],
                                            start=True, stop=True)
                        return f
                    p.op("tensor", mk_lat(), reads=["rkT", "rqT"], writes=[bank])
                    p.op("scalar", lambda e, lh=lh, bb=bb: e.activation(
                        platm[bb][0:64, lh * 256:(lh + 1) * 256], mm[lh % 2][0:64, 0:256],
                        AF.Exp, scale=cc_sb[0:64, CC_SCL:CC_SCL + 1]),
                        reads=[bank, "cc_sb"], writes=[f"platm_{bb}"] if lh == 0 else ())
                    if lh > 0:
                        p.r(f"platm_{bb}").w = (p.esem["scalar"], p.esem["scalar"].n)
                def mk_lmask(bb=bb, b=b):
                    def f(e):
                        ins = None
                        for lh in range(4):
                            ins = e.tensor_mul(platm[bb][0:64, lh * 256:(lh + 1) * 256],
                                               platm[bb][0:64, lh * 256:(lh + 1) * 256],
                                               latmask16[:, b * 256:(b + 1) * 256])
                        return ins
                    return f
                p.op("vector", mk_lmask(), reads=[f"platm_{bb}", "latmask16"], writes=[f"platm_{bb}"])
                for lh in range(4):
                    pr, par = lh // 2, lh % 2

                    def mk_latden(lh=lh, bb=bb, first=(lh == 0)):
                        def f(e):
                            return e.matmul(pk[bb][0:16, 256:512], onescol16[0:64, lh * 16:(lh + 1) * 16],
                                            platm[bb][0:64, lh * 256:(lh + 1) * 256],
                                            start=first, stop=(lh == 3))
                        return f
                    p.op("tensor", mk_latden(), reads=[f"platm_{bb}", "onescol16"],
                         writes=[pkb])
                def mk_latpv(bb=bb):
                    def f(e):
                        ins = None
                        for lh in range(4):
                            pr, par = lh // 2, lh % 2
                            ins = e.matmul(mm[2][par * 64:(par + 1) * 64, pr * 256:(pr + 1) * 256],
                                           rv_sb[:, lh * 64:(lh + 1) * 64],
                                           platm[bb][0:64, lh * 256:(lh + 1) * 256],
                                           start=True, stop=True)
                        return ins
                    return f
                p.op("tensor", mk_latpv(), reads=[f"platm_{bb}", "rv_sb"], writes=[MMR[2]])
                p.op("vector", lambda e, bb=bb: e.tensor_scalar(rlatf[:], pk[bb][0:4, 256:512], 1e-30, None, ALU.max),
                     reads=[pkb], writes=["rlatf"])
                p.op("vector", lambda e: e.reciprocal(rlatf[:], rlatf[:]), reads=["rlatf"], writes=["rlatf"])
                p.op("vector", lambda e: e.tensor_copy(rlat16[:], rlatf[:]), reads=["rlatf"], writes=["rlat16"])
                def mk_latbc(e):
                    i1 = e.matmul(mm[0][:, 0:256], latpairsel16[:, 0:128], rlat16[:], start=True, stop=True)
                    i2 = e.matmul(mm[0][:, 256:512], latpairsel16[:, 128:256], rlat16[:], start=True, stop=True)
                    return i2
                p.op("tensor", mk_latbc, reads=["rlat16", "latpairsel16"], writes=[MMR[0]])
                p.op("scalar", lambda e, bb=bb: e.activation(pexp[bb][:], mm[2][:, 0:512], AF.Identity),
                     reads=[MMR[2]], writes=[f"pexp_{bb}"])
                for pr in range(2):
                    p.op("vector", lambda e, pr=pr, bb=bb: e.tensor_mul(
                        latn[bb][:, pr * 256:(pr + 1) * 256], pexp[bb][:, pr * 256:(pr + 1) * 256], mm[0][:, pr * 256:(pr + 1) * 256]),
                        reads=[f"pexp_{bb}", MMR[0]], writes=[f"latn_{bb}"] if pr == 0 else ())
                p.r(f"latn_{bb}").w = (p.esem["vector"], p.esem["vector"].n)
                for pr in range(8):
                    for wi, base in ((0, 0), (1, 1024)):
                        bidx = 1 + (2 * pr + wi) % 2
                        bank = MMR[bidx]

                        def mk_qk(pr=pr, base=base, bb=bb, bidx=bidx):
                            def f(e):
                                ins = None
                                for k in range(8):
                                    ins = e.matmul(mm[bidx][:, 0:256],
                                                   qkv_w_sb[:, k * 3072 + base + pr * 128:k * 3072 + base + (pr + 1) * 128],
                                                   xtb[bb][:, k * 256:(k + 1) * 256], start=(k == 0), stop=(k == 7))
                                return ins
                            return f
                        p.op("tensor", mk_qk(), reads=[f"xtb_{bb}", "qkv_w"], writes=[bank])
                        if wi == 0:
                            p.op("scalar", lambda e, pr=pr, bb=bb, bidx=bidx: e.activation(
                                qt_sb[bb][:, pr * 256:(pr + 1) * 256], mm[bidx][:, 0:256], AF.Identity,
                                bias=qkv_b_pm[:, pr:pr + 1]), reads=[bank, "qkv_b_pm_s"],
                                writes=[f"qt_{bb}"] if pr == 0 else ())
                            if pr > 0:
                                p.r(f"qt_{bb}").w = (p.esem["scalar"], p.esem["scalar"].n)
                        else:
                            s0, s1 = 2 * b % 4, (2 * b + 1) % 4

                            def mk_kev(pr=pr, s0=s0, s1=s1, bidx=bidx):
                                def f(e):
                                    e.activation(kt_ring[:, pr * 512 + s0 * 128:pr * 512 + s0 * 128 + 128],
                                                 mm[bidx][:, 0:128], AF.Identity, bias=qkv_b_pm[:, 8 + pr:9 + pr])
                                    return e.activation(kt_ring[:, pr * 512 + s1 * 128:pr * 512 + s1 * 128 + 128],
                                                        mm[bidx][:, 128:256], AF.Identity, bias=qkv_b_pm[:, 8 + pr:9 + pr])
                                return f
                            p.op("scalar", mk_kev(), reads=[bank, "qkv_b_pm_s"],
                                 writes=[f"ktr{s0}", f"ktr{s1}"] if pr == 0 else ())
                            if pr > 0:
                                ident_ = (p.esem["scalar"], p.esem["scalar"].n)
                                p.r(f"ktr{s0}").w = ident_
                                p.r(f"ktr{s1}").w = ident_
                for t in range(2):
                    slot = (2 * b + t) % 4

                    def mk_v(t=t, bb=bb):
                        def f(e):
                            ins = None
                            for k in range(8):
                                for half in range(2):
                                    ins = e.matmul(out2[:, half * 512:(half + 1) * 512],
                                                   xtb[bb][:, k * 256 + t * 128:k * 256 + (t + 1) * 128],
                                                   qkv_w_sb[:, k * 3072 + 2048 + half * 512:k * 3072 + 2048 + (half + 1) * 512],
                                                   start=(k == 0), stop=False)
                            ins = e.matmul(out2[:, 0:512], onesrow16[:], qkv_b_vrow[0:1, 0:512], start=False, stop=True)
                            ins = e.matmul(out2[:, 512:1024], onesrow16[:], qkv_b_vrow[0:1, 512:1024], start=False, stop=True)
                            return ins
                        return f
                    p.op("tensor", mk_v(), reads=[f"xtb_{bb}", "qkv_w", "onesrow16", "qkv_b_vrow"], writes=["out2", "out2a", "out2b"])
                    p.op("vector", lambda e, slot=slot: e.tensor_copy(v_ring[:, slot * 1024:(slot + 1) * 1024], out2[:]),
                         reads=["out2", "out2a", "out2b"], writes=[f"vr{slot}"])

                for ds in range(8):
                    bank = MMR[1 + ds % 2]

                    def mk_rout(ds=ds, bb=bb, bank=1 + ds % 2):
                        def f(e):
                            ins = None
                            for k in range(2):
                                ins = e.matmul(mm[bank][:, 0:256], rout_w_sb[:, k * 1024 + ds * 128:k * 1024 + (ds + 1) * 128],
                                               latn[bb][:, k * 256:(k + 1) * 256], start=(k == 0), stop=(k == 1))
                            return ins
                        return f
                    p.op("tensor", mk_rout(), reads=[f"latn_{bb}", "rout_w"], writes=[MMR[1 + ds % 2]])
                    p.op("scalar", lambda e, ds=ds, bb=bb, bank=1 + ds % 2: e.activation(
                        remoteT[bb][:, ds * 256:(ds + 1) * 256], mm[bank][:, 0:256], AF.Identity,
                        bias=rout_b_pm[:, ds:ds + 1]), reads=[MMR[1 + ds % 2], "rout_b_pm_s"],
                        writes=[f"remoteT_{bb}"] if ds == 0 else ())
                p.r(f"remoteT_{bb}").w = (p.esem["scalar"], p.esem["scalar"].n)

                sA, sB, sC_ = (2 * b - 1) % 4, 2 * b % 4, (2 * b + 1) % 4
                for h in range(16):
                    pr, par = h // 2, h % 2
                    bank = MMR[h % 3]
                    pxn, pmn = f"pexp_{h % 2}", f"pmask_{h % 2}"

                    def mk_sc(pr=pr, par=par, bb=bb, bidx=h % 3, sA=sA, sB=sB, sC_=sC_):
                        def f(e):
                            e.matmul(mm[bidx][:, 0:128],
                                     kt_ring[par * 64:(par + 1) * 64, pr * 512 + sA * 128:pr * 512 + (sA + 1) * 128],
                                     qt_sb[bb][par * 64:(par + 1) * 64, pr * 256:pr * 256 + 128], start=True, stop=True)
                            e.matmul(mm[bidx][:, 128:384],
                                     kt_ring[par * 64:(par + 1) * 64, pr * 512 + sB * 128:pr * 512 + (sB + 1) * 128],
                                     qt_sb[bb][par * 64:(par + 1) * 64, pr * 256:(pr + 1) * 256], start=True, stop=True)
                            return e.matmul(mm[bidx][:, 384:512],
                                            kt_ring[par * 64:(par + 1) * 64, pr * 512 + sC_ * 128:pr * 512 + (sC_ + 1) * 128],
                                            qt_sb[bb][par * 64:(par + 1) * 64, pr * 256 + 128:(pr + 1) * 256], start=True, stop=True)
                        return f
                    p.op("tensor", mk_sc(), reads=[f"ktr{sA}", f"ktr{sB}", f"ktr{sC_}", f"qt_{bb}"], writes=[bank])
                    p.op("scalar", lambda e, bidx=h % 3, pidx=h % 2: e.activation(pexp[pidx][:], mm[bidx][:], AF.Exp,
                                                                      scale=cc_sb[:, CC_SCL:CC_SCL + 1]),
                         reads=[bank, "cc_sb"], writes=[pxn])
                    p.op("vector", lambda e, pidx=h % 2, msk=msk: e.tensor_mul(pmask[pidx][:], pexp[pidx][:], msk[:]),
                         reads=[pxn, mskn], writes=[pmn])

                    def mk_den(h=h, bidx=h % 2, bb=bb):
                        def f(e):
                            ins = None
                            segs = [(0, 128, 0, 128), (128, 384, 0, 256), (384, 512, 128, 256)]
                            if h == 0:
                                segs = [segs[1], segs[0], segs[2]]
                            for si, (c0, c1, q0, q1) in enumerate(segs):
                                ins = e.matmul(pk[bb][0:16, q0:q1], onescol16[:, h * 16:(h + 1) * 16],
                                               pmask[bidx][:, c0:c1], start=(h == 0 and si == 0),
                                               stop=(h == 15 and c0 == 384))
                            return ins
                        return f
                    p.op("tensor", mk_den(), reads=[pmn, "onescol16"], writes=[pkb])
                    pvbank = ["out2a", "out2b"][h % 2]
                    pvoff = [0, 512][h % 2]

                    def mk_pv(h=h, par=par, pidx=h % 2, pvoff=pvoff, sA=sA, sB=sB, sC_=sC_):
                        def f(e):
                            segs = [(sB, 128, 384, 0, 256), (sA, 0, 128, 0, 128), (sC_, 384, 512, 128, 256)]
                            ins = None
                            fi = True
                            for (slot, c0, c1, q0, q1) in segs:
                                ins = e.matmul(out2[par * 64:(par + 1) * 64, pvoff + q0:pvoff + q1],
                                               v_ring[:, slot * 1024 + h * 64:slot * 1024 + (h + 1) * 64],
                                               pmask[pidx][:, c0:c1], start=fi, stop=(c0 == 384))
                                fi = False
                            return ins
                        return f
                    p.op("tensor", mk_pv(), reads=[pmn, f"vr{sA}", f"vr{sB}", f"vr{sC_}"], writes=[pvbank])
                    p.op("vector", lambda e, pr=pr, par=par, bb=bb, pvoff=pvoff: e.tensor_copy(
                        localT[bb][par * 64:(par + 1) * 64, pr * 256:(pr + 1) * 256], out2[par * 64:(par + 1) * 64, pvoff:pvoff + 256]),
                        reads=[pvbank], writes=[f"localT_{bb}"] if h == 0 else ())
                    if h > 0:
                        p.r(f"localT_{bb}").w = (p.esem["vector"], p.esem["vector"].n)
                def mk_gpack(pat, srcT, rows, bb=bb):
                    def f(e):
                        ins = None
                        for h2 in range(16):
                            ins = e.matmul(pk[bb][rows[0]:rows[1], 0:256], pat[:, h2 * 16:(h2 + 1) * 16],
                                           srcT[:, (h2 // 2) * 256:(h2 // 2 + 1) * 256],
                                           start=(h2 == 0), stop=(h2 == 15))
                        return ins
                    return f
                p.op("tensor", mk_gpack(glpat16, localT[bb], (32, 48)),
                     reads=[f"localT_{bb}", "glpat16"], writes=[pkb])
                p.op("tensor", mk_gpack(grpat16, remoteT[bb], (64, 80)),
                     reads=[f"remoteT_{bb}", "grpat16"], writes=[pkb])

                p.op("vector", lambda e, bb=bb: e.tensor_scalar(rloc[:], pk[bb][0:16, 0:256], 1e-30, None, ALU.max),
                     reads=[pkb], writes=["rloc"])
                p.op("vector", lambda e: e.reciprocal(rloc[:], rloc[:]), reads=["rloc"], writes=["rloc"])
                p.op("vector", lambda e, bb=bb: e.tensor_mul(gtmp[:], pk[bb][32:48, 0:256], rloc[:]),
                     reads=[pkb, "rloc"], writes=["gtmp"])
                p.op("vector", lambda e, bb=bb: e.tensor_add(gtmp[:], gtmp[:], pk[bb][64:80, 0:256]),
                     reads=["gtmp", pkb], writes=["gtmp"])
                p.op("scalar", lambda e: e.activation(gsig[:], gtmp[:], AF.Sigmoid), reads=["gtmp"], writes=["gsig"])
                p.op("vector", lambda e: e.tensor_mul(gtmp[:], gsig[:], rloc[:]), reads=["gsig", "rloc"], writes=["gtmp"])
                p.op("vector", lambda e: e.tensor_copy(apack[:], gtmp[:]), reads=["gtmp"], writes=["apack"])
                p.op("vector", lambda e: e.tensor_scalar(bpack[:], gsig[:], -1.0, 1.0, ALU.mult, ALU.add),
                     reads=["gsig"], writes=["bpack"])

                for pr in range(8):
                    bci = pr % 3
                    def mk_bc(pr=pr, bci=bci):
                        def f(e):
                            e.matmul(mm[bci][:, 0:256], pairsel16[:, pr * 128:(pr + 1) * 128], apack[:], start=True, stop=True)
                            return e.matmul(mm[bci][:, 256:512], pairsel16[:, pr * 128:(pr + 1) * 128], bpack[:], start=True, stop=True)
                        return f
                    p.op("tensor", mk_bc(), reads=["apack", "bpack", "pairsel16"], writes=[MMR[bci]])
                    mt = mxt[pr % 2]
                    p.op("vector", lambda e, pr=pr, bb=bb, mt=mt, bci=bci: e.tensor_mul(
                        mt[:], remoteT[bb][:, pr * 256:(pr + 1) * 256], mm[bci][:, 256:512]),
                        reads=[f"remoteT_{bb}", MMR[bci]], writes=[f"mxt_{pr % 2}"])
                    p.op("vector", lambda e, pr=pr, bb=bb, bci=bci: e.tensor_mul(
                        mixedT[bb][:, pr * 256:(pr + 1) * 256], localT[bb][:, pr * 256:(pr + 1) * 256], mm[bci][:, 0:256]),
                        reads=[f"localT_{bb}", MMR[bci]], writes=[f"mixedT_{bb}"] if pr == 0 else ())
                    p.r(f"mixedT_{bb}").w = (p.esem["vector"], p.esem["vector"].n)
                    p.op("vector", lambda e, pr=pr, bb=bb, mt=mt: e.tensor_add(
                        mixedT[bb][:, pr * 256:(pr + 1) * 256], mixedT[bb][:, pr * 256:(pr + 1) * 256], mt[:]),
                        reads=[f"mxt_{pr % 2}", f"mixedT_{bb}"], writes=())
                    p.r(f"mixedT_{bb}").w = (p.esem["vector"], p.esem["vector"].n)

                for t in range(2):
                    def mk_op(t=t, bb=bb):
                        def f(e):
                            ins = None
                            for k in range(8):
                                for half in range(2):
                                    ins = e.matmul(out2[:, half * 512:(half + 1) * 512],
                                                   mixedT[bb][:, k * 256 + t * 128:k * 256 + (t + 1) * 128],
                                                   out_w_sb[:, k * 1024 + half * 512:k * 1024 + (half + 1) * 512],
                                                   start=(k == 0), stop=False)
                            ins = e.matmul(out2[:, 0:512], onesrow16[:], out_b_row16[0:1, 0:512], start=False, stop=True)
                            ins = e.matmul(out2[:, 512:1024], onesrow16[:], out_b_row16[0:1, 512:1024], start=False, stop=True)
                            return ins
                        return f
                    p.op("tensor", mk_op(), reads=[f"mixedT_{bb}", "out_w", "onesrow16", "out_b_row16"], writes=["out2", "out2a", "out2b"])
                    yb, ybn = y_sb[t], f"y_sb_{t}"
                    lb, lbn = lny_sb[t], f"lny_sb_{t}"
                    p.op("vector", lambda e, yb=yb, t=t: e.tensor_add(yb[:], out2[:], x3[t][:]),
                         reads=["out2", "out2a", "out2b", f"x3_{t}"], writes=[ybn])
                    sB, N = ln_stats(ybn, yb, (lb, lbn), par=t)
                    p.op("scalar", lambda e, yb=yb, lb=lb, sB=sB: e.activation(lb[:], yb[:], AF.Identity, bias=sB[2][:], scale=sB[5][:]),
                         reads=[ybn, N[2], N[5]], writes=[lbn])
                    p.op("vector", lambda e, lb=lb, sB=sB: e.tensor_reduce(sB[0][:], lb[:], AX.X, ALU.max, apply_absolute_value=True),
                         reads=[lbn], writes=[N[0]])
                    if not lnmax_init[0]:
                        p.op("vector", lambda e, sB=sB: e.tensor_copy(lnmax[:], sB[0][:]), reads=[N[0]], writes=["lnmax"])
                        lnmax_init[0] = True
                    else:
                        p.op("vector", lambda e, sB=sB: e.tensor_max(lnmax[:], lnmax[:], sB[0][:]), reads=[N[0]], writes=["lnmax"])
                    row = b * 256 + t * 128
                    dma("sync", y_spill[q, row:row + 128, :], yb[:], "ySp%d" % t, reads=[ybn])
                    dma("sync", lny_spill[q, row:row + 128, :], lb[:], "lSp%d" % t, reads=[lbn])
        nc._mark_attn = len(p.ops)
        ATT_BAR = bar()

        p.op("tensor", lambda e: e.transpose(mm[0][0:1, 0:128], lnmax[:], identf[:]),
             reads=["lnmax", "identf"], writes=[MMR[0]], waits=[ATT_BAR])
        p.op("vector", lambda e: e.reduce_max(scal[0:1, 0:1], mm[0][0:1, 0:128], AX.X),
             reads=[MMR[0]], writes=["scal"])
        p.op("gpsimd", lambda e: e.dma_start(out=cc1_in, in_=scal[0:1, 0:1]),
             reads=["scal"], sem=dsems["dCC"], amt=16)
        if ncores == 1:
            p.op("gpsimd", lambda e: e.dma_start(out=cc1_out, in_=cc1_in),
                 waits=[aft(dsems["dCC"])], sem=dsems["dCC"], amt=16)
            p.op("gpsimd", lambda e: e.memset(barcell[:], 0.0), waits=[aft(dsems["dCC"])], writes=["barcell"])
        else:
            p.op("gpsimd", lambda e: e.collective_compute(
                "AllReduce", ALU.max, replica_groups=[list(range(ncores))],
                ins=[cc1_in], outs=[cc1_out]), waits=[aft(dsems["dCC"])])
        p.op("gpsimd", lambda e: e.dma_start(out=scal[0:1, 0:1], in_=cc1_out),
             waits=[aft(p.esem["gpsimd"])], writes=["scal"], sem=dsems["dCC"], amt=16)
        p.op("vector", lambda e: e.tensor_scalar(scal[0:1, 1:2], scal[0:1, 0:1], 1e-6, 1.0 / 127.0, ALU.max, ALU.mult),
             reads=["scal"], writes=["scal"])
        p.op("vector", lambda e: e.reciprocal(scal[0:1, 2:3], scal[0:1, 1:2]), reads=["scal"], writes=["scal"])
        p.op("vector", lambda e: e.tensor_mul(scal[0:1, 3:4], scal[0:1, 1:2], wscal[0:1, 1:2]),
             reads=["scal", "wscal"], writes=["scal"])
        bcast_scalar(scal[0:1, 2:3], "scal", rs_b, "rs_b")
        bcast_scalar(scal[0:1, 3:4], "scal", ss1_b, "ss1_b")
        p.op("vector", lambda e: e.tensor_scalar(b1s[:], b1_pm[:], ss1_b[:, 0:1], None, ALU.mult),
             reads=["b1_pm_s", "ss1_b"], writes=["b1s"])
        for kt in range(8):
            for ci in range(2):
                idx = kt * 2 + ci
                buf, bn = wstage[idx % 2], f"wstage_{idx % 2}"
                dma("sync", buf[:], di["w1t"][kt * 128:(kt + 1) * 128, ci * 2048:(ci + 1) * 2048],
                    "dWq%d" % (idx % 2), writes=[bn], waits=[ATT_BAR])
                p.op("scalar", lambda e, buf=buf: e.activation(buf[:], buf[:], AF.Identity,
                                                               scale=rs1_b[:, 0:1], bias=cc_sb[:, CC_MAG:CC_MAG + 1]),
                     reads=[bn, "rs1_b", "cc_sb"], writes=[bn])
                p.op("vector", lambda e, buf=buf, kt=kt, ci=ci: e.tensor_scalar(
                    wq_sb[:, kt * 4096 + ci * 2048:kt * 4096 + (ci + 1) * 2048], buf[:], -MAGIC, None, ALU.add),
                    reads=[bn], writes=["wq_sb"] if idx == 0 else ())
        p.r("wq_sb").w = (p.esem["vector"], p.esem["vector"].n)

        hmax_init = [False]
        NG = NTOK // 512
        for g in range(NG):
            q, gg = divmod(g, 4)
            xq, xqn = xqT[g % 2], f"xqT_{g % 2}"
            for t in range(4):
                row = gg * 512 + t * 128
                qt_, qtn = qtile[t % 2], f"qtile_{t % 2}"
                dma("sync", qt_[:], lny_spill[q, row:row + 128, :], "dQ%d" % (t % 2), writes=[qtn], waits=[ATT_BAR])
                p.op("scalar", lambda e, qt_=qt_: e.activation(qt_[:], qt_[:], AF.Identity,
                                                               scale=rs_b[:, 0:1], bias=cc_sb[:, CC_MAG:CC_MAG + 1]),
                     reads=[qtn, "rs_b", "cc_sb"], writes=[qtn])
                for half in range(2):
                    def mk_trq(qt_=qt_, half=half):
                        def f(e):
                            ins = None
                            for k in range(4):
                                ins = e.transpose(pk[half][:, k * 128:(k + 1) * 128],
                                                  qt_[:, half * 512 + k * 128:half * 512 + (k + 1) * 128], identf[:])
                            return ins
                        return f
                    p.op("tensor", mk_trq(), reads=[qtn, "identf"], writes=[PKR[half]])
                    p.op("vector", lambda e, xq=xq, t=t, half=half: e.tensor_scalar(
                        xq[:].rearrange("p (k s) -> p k s", k=8)[:, half * 4:(half + 1) * 4, t * 128:(t + 1) * 128],
                        pk[half][:].rearrange("p (k s) -> p k s", k=4), -MAGIC, None, ALU.add),
                        reads=[PKR[half]], writes=[xqn] if (t == 0 and half == 0) else ())
                    if not (t == 0 and half == 0):
                        p.r(xqn).w = (p.esem["vector"], p.esem["vector"].n)
            for m_ in range(32):
                bidx = m_ % 3

                def mk_f1(m_=m_, xq=xq, bidx=bidx):
                    def f(e):
                        ins = None
                        for k in range(8):
                            ins = e.matmul(mm[bidx][:, 0:512], wq_sb[:, k * 4096 + m_ * 128:k * 4096 + (m_ + 1) * 128],
                                           xq[:, k * 512:(k + 1) * 512], start=(k == 0), stop=(k == 7))
                        return ins
                    return f
                p.op("tensor", mk_f1(), reads=[xqn, "wq_sb"], writes=[MMR[bidx]])
                hb, hbn = hT_st[m_ % 2], f"hT_st_{m_ % 2}"
                p.op("scalar", lambda e, m_=m_, hb=hb, bidx=bidx: e.activation(
                    hb[:], mm[bidx][:, 0:512], AF.Gelu, scale=ss1_b[:, 0:1], bias=b1s[:, m_:m_ + 1]),
                    reads=[MMR[bidx], "ss1_b", "b1s"], writes=[hbn])
                p.op("vector", lambda e, hb=hb: e.tensor_reduce(st[0][:], hb[:], AX.X, ALU.max, apply_absolute_value=True),
                     reads=[hbn], writes=["st0_"])
                if not hmax_init[0]:
                    p.op("vector", lambda e: e.tensor_copy(hmax[:], st[0][:]), reads=["st0_"], writes=["hmax"])
                    hmax_init[0] = True
                else:
                    p.op("vector", lambda e: e.tensor_max(hmax[:], hmax[:], st[0][:]), reads=["st0_"], writes=["hmax"])
                dma("scalar", h_spill[m_, :, g * 512:(g + 1) * 512], hb[:], "hS%d" % (m_ % 2), reads=[hbn])
        nc._mark_ffna = len(p.ops)
        FFNA_BAR = bar()

        p.op("tensor", lambda e: e.transpose(mm[0][0:1, 0:128], hmax[:], identf[:]),
             reads=["hmax", "identf"], writes=[MMR[0]], waits=[FFNA_BAR])
        p.op("vector", lambda e: e.reduce_max(scal[0:1, 4:5], mm[0][0:1, 0:128], AX.X),
             reads=[MMR[0]], writes=["scal"])
        p.op("gpsimd", lambda e: e.dma_start(out=cc2_in, in_=scal[0:1, 4:5]),
             reads=["scal"], sem=dsems["dCC"], amt=16)
        if ncores == 1:
            p.op("gpsimd", lambda e: e.dma_start(out=cc2_out, in_=cc2_in),
                 waits=[aft(dsems["dCC"])], sem=dsems["dCC"], amt=16)
            p.op("gpsimd", lambda e: e.memset(barcell[:], 0.0), waits=[aft(dsems["dCC"])], writes=["barcell"])
        else:
            p.op("gpsimd", lambda e: e.collective_compute(
                "AllReduce", ALU.max, replica_groups=[list(range(ncores))],
                ins=[cc2_in], outs=[cc2_out]), waits=[aft(dsems["dCC"])])
        p.op("gpsimd", lambda e: e.dma_start(out=scal[0:1, 4:5], in_=cc2_out),
             waits=[aft(p.esem["gpsimd"])], writes=["scal"], sem=dsems["dCC"], amt=16)
        p.op("vector", lambda e: e.tensor_scalar(scal[0:1, 5:6], scal[0:1, 4:5], 1e-6, 1.0 / 127.0, ALU.max, ALU.mult),
             reads=["scal"], writes=["scal"])
        p.op("vector", lambda e: e.reciprocal(scal[0:1, 6:7], scal[0:1, 5:6]), reads=["scal"], writes=["scal"])
        p.op("vector", lambda e: e.tensor_mul(scal[0:1, 7:8], scal[0:1, 5:6], wscal[0:1, 4:5]),
             reads=["scal", "wscal"], writes=["scal"])
        bcast_scalar(scal[0:1, 6:7], "scal", rsh_b, "rsh_b")
        bcast_scalar(scal[0:1, 7:8], "scal", ss2_b, "ss2_b")
        for kt in range(32):
            buf, bn = wstage[kt % 2], f"wstage_{kt % 2}"
            dma("sync", buf[:, 0:1024], di["w2t"][kt * 128:(kt + 1) * 128, :], "dWq%d" % (kt % 2),
                writes=[bn], waits=[FFNA_BAR])
            p.op("scalar", lambda e, buf=buf: e.activation(buf[:, 0:1024], buf[:, 0:1024], AF.Identity,
                                                           scale=rs2_b[:, 0:1], bias=cc_sb[:, CC_MAG:CC_MAG + 1]),
                 reads=[bn, "rs2_b", "cc_sb"], writes=[bn])
            p.op("vector", lambda e, buf=buf, kt=kt: e.tensor_scalar(
                wq_sb[:, kt * 1024:(kt + 1) * 1024], buf[:, 0:1024], -MAGIC, None, ALU.add),
                reads=[bn], writes=["wq_sb"] if kt == 0 else ())
        p.r("wq_sb").w = (p.esem["vector"], p.esem["vector"].n)

        for g2 in range(NTOK // 256):
            q, gg = divmod(g2, 8)
            hq, hqn = hq_sb2[g2 % 2], f"hq_sb_{g2 % 2}"
            for k in range(32):
                sl = k % 4
                buf = hload[sl // 2]
                boff = (sl % 2) * 256
                bn = f"hload_{sl}"
                p.op("scalar", lambda e, buf=buf, boff=boff, k=k, g2=g2: e.dma_start(
                    out=buf[:, boff:boff + 256], in_=h_spill[k, :, g2 * 256:(g2 + 1) * 256]),
                    writes=[bn], waits=[FFNA_BAR], sem=dsems["dH%d" % sl], amt=16)
                p.op("vector", lambda e, buf=buf, boff=boff: e.tensor_scalar(
                    hldt[:, 0:256], buf[:, boff:boff + 256], rsh_b[:, 0:1], MAGIC, ALU.mult, ALU.add),
                    reads=[bn, "rsh_b"], writes=["hldt"])
                p.op("vector", lambda e, k=k, hq=hq: e.tensor_scalar(
                    hq[:, k * 256:(k + 1) * 256], hldt[:, 0:256], -MAGIC, None, ALU.add),
                    reads=["hldt"], writes=[hqn] if k == 0 else ())
            p.r(hqn).w = (p.esem["vector"], p.esem["vector"].n)
            for t in range(2):
                row = gg * 256 + t * 128
                yb, ybn = y_sb[t % 2], f"y_sb_{t % 2}"
                dma("sync", yb[:], y_spill[q, row:row + 128, :], "dY%d" % (t % 2), writes=[ybn], waits=[ATT_BAR])

                def mk_f2(t=t, hq=hq):
                    def f(e):
                        ins = None
                        for k in range(32):
                            for half in range(2):
                                ins = e.matmul(out2[:, half * 512:(half + 1) * 512],
                                               hq[:, k * 256 + t * 128:k * 256 + (t + 1) * 128],
                                               wq_sb[:, k * 1024 + half * 512:k * 1024 + (half + 1) * 512],
                                               start=(k == 0), stop=False)
                        ins = e.matmul(out2[:, 0:512], onesrow16[:], b2_row16[0:1, 0:512], start=False, stop=True)
                        ins = e.matmul(out2[:, 512:1024], onesrow16[:], b2_row16[0:1, 512:1024], start=False, stop=True)
                        return ins
                    return f
                p.op("tensor", mk_f2(), reads=[hqn, "wq_sb", "onesrow16", "b2_row16"], writes=["out2", "out2a", "out2b"])
                ob, obn = oout[t % 2], f"oout_{t % 2}"
                p.op("vector", lambda e, ob=ob, yb=yb: e.scalar_tensor_tensor(
                    ob[:], out2[:], ss2_b[:, 0:1], yb[:], ALU.mult, ALU.add),
                    reads=["out2", "out2a", "out2b", "ss2_b", ybn], writes=[obn])
                dma("sync", out_d[q, row:row + 128, :], ob[:], "oS%d" % (t % 2), reads=[obn])
        p.op("sync", None, waits=[aft(s_) for s_ in dsems.values()])
        p.op("gpsimd", None, waits=[aft(dsems["oS0"]), aft(dsems["oS1"])])

        if trunc is not None:
            p.ops = p.ops[:trunc]
        with nc.Block() as block:
            p.emit(block)

    return nc


def kernel(**inputs):
    x = np.ascontiguousarray(np.asarray(inputs["L0_attn_x"], dtype=np.float32))
    consts = build_host_consts()
    wz = build_weight_inputs(inputs)
    nseq = 4
    nc = build_program(nseq=nseq)
    in_maps = []
    for c in range(NCORES):
        m = dict(consts)
        m.update(wz)
        m["x"] = np.ascontiguousarray(x[c * nseq:(c + 1) * nseq])
        in_maps.append(m)
    res = run_bass_kernel_spmd(nc, in_maps, core_ids=list(range(NCORES)))
    out = np.concatenate([np.asarray(r["out"]) for r in res.results], axis=0)
    return out.reshape(32, 2048, 1024)
